# version 07
# baseline (speedup 1.0000x reference)
import math
import numpy as np
import ml_dtypes
import concourse.bass as bass
import concourse.mybir as mybir
from concourse.alu_op_type import AluOpType as OP
from concourse.bass_utils import run_bass_kernel_spmd

F32 = mybir.dt.float32
BF16 = mybir.dt.bfloat16
I32 = mybir.dt.int32
AF = mybir.ActivationFunctionType
AX = mybir.AxisListType

BFNP = ml_dtypes.bfloat16

NCORES = 8
B, L, D = 16, 2048, 1024
SEQ_PER_CORE = 2
TILE = 512
NTILES = L // TILE
TC = TILE // 128
NI = 2816
NH = 4
DK = 256
FFI = 4096
N_SLICE = 512
NSLICES = (NI + N_SLICE - 1) // N_SLICE

TWO_PI = 2.0 * math.pi


class Prog:
    CE = ("T", "V", "A", "G")

    def __init__(self):
        self.ops = {e: [] for e in ("T", "V", "A", "G", "S")}
        self.cnt = {e: 0 for e in self.CE}
        self.dmacnt = {}
        self.dirtyW = {e: set() for e in ("T", "V", "A", "G")}
        self.dirtyR = {e: set() for e in ("T", "V", "A", "G")}
        self.lastwait = {e: {} for e in ("T", "V", "A", "G", "S")}
        self.sems = {}
        self.writer = {}
        self.readers = {}
        self.need_inc = {e: set() for e in self.CE}
        self.inc_cell = {e: {} for e in self.CE}
        self.resolved = {e: {} for e in self.CE}
        self.max_waited = {}
        self.qsynced = {}
        self.dma_bound = {}
        self.dma_resolved = {}

    @staticmethod
    def ap(x):
        if x is None or isinstance(x, (int, float)):
            return x
        if hasattr(x, "then_inc"):
            return x
        try:
            import bass_rust
            if isinstance(x, bass_rust.AP):
                return x
        except Exception:
            pass
        if hasattr(x, "ap") and callable(getattr(x, "ap", None)):
            return x.ap()
        return x

    def _names(self, aps):
        out = []
        for a in aps:
            if a is None:
                continue
            try:
                out.append(a.tensor.name)
            except Exception:
                pass
        return out

    def wait(self, e, sem, val):
        if val <= 0:
            return
        lw = self.lastwait[e]
        if lw.get(sem, -1) >= val:
            return
        lw[sem] = val
        if sem in self.CE:
            self.need_inc[sem].add(val)
            self.ops[e].append(lambda eng, s=sem, v=val: eng.wait_ge(self.sems[s], self.resolved[s][v]))
        else:
            self.max_waited[sem] = max(self.max_waited.get(sem, 0), val)
            self.ops[e].append(lambda eng, s=sem, v=val: eng.wait_ge(
                self.sems[s], self.dma_resolved.get(s, {}).get(v, v)))

    def drain(self, e):
        if self.dirtyW[e] or self.dirtyR[e]:
            self.dirtyW[e] = set()
            self.dirtyR[e] = set()
            self.ops[e].append(lambda eng: eng.drain())

    def _dep_waits(self, e, rd, wr):
        for r in rd:
            w = self.writer.get(r)
            if w and w[0] != e:
                self.wait(e, w[0], w[1])
        for w_ in wr:
            lw = self.writer.get(w_)
            if lw and lw[0] != e:
                self.wait(e, lw[0], lw[1])
            for re_, rc in self.readers.get(w_, {}).items():
                if re_ != e:
                    self.wait(e, re_, rc)

    def _dep_update(self, e, rd, wr, val):
        for w_ in wr:
            self.writer[w_] = (e, val)
            self.readers[w_] = {}
        for r in rd:
            self.readers.setdefault(r, {})[e] = val

    def op(self, e, fn, reads=(), writes=(), inc=True, rdaps=(), wraps=()):
        rd = set(self._names(rdaps)) | set(reads)
        wr = set(self._names(wraps)) | set(writes)
        assert e in self.CE
        self._dep_waits(e, rd, wr)
        if (rd & self.dirtyW[e]) or (wr & (self.dirtyW[e] | self.dirtyR[e])):
            self.drain(e)
        self.dirtyW[e] |= wr
        self.dirtyR[e] |= rd
        self.cnt[e] += 1
        c = self.cnt[e]
        self._dep_update(e, rd, wr, c)
        cell = [False]
        self.inc_cell[e][c] = cell
        def run(eng, f=fn, s=e, _cell=cell):
            ins = f(eng)
            if _cell[0]:
                ins.then_inc(self.sems[s], 1)
            return ins
        self.ops[e].append(run)

    def qwait(self, qe, sem, val):
        if val <= 0:
            return
        lw = self.lastwait[qe]
        if lw.get(sem, -1) >= val:
            return
        lw[sem] = val
        if sem in self.CE:
            self.need_inc[sem].add(val)
            self.ops[qe].append(lambda eng, s=sem, v=val: eng.wait_ge(self.sems[s], self.resolved[s][v]))
        else:
            self.max_waited[sem] = max(self.max_waited.get(sem, 0), val)
            self.ops[qe].append(lambda eng, s=sem, v=val: eng.wait_ge(
                self.sems[s], self.dma_resolved.get(s, {}).get(v, v)))

    def load(self, dst_ap, src_ap, sem, qe="S", slow=False):
        dn = self._names([dst_ap]); sn = self._names([src_ap])
        for w_ in dn:
            lw = self.writer.get(w_)
            if lw and lw[0] != sem:
                self.qwait(qe, lw[0], lw[1])
            for re_, rc in self.readers.get(w_, {}).items():
                self.qwait(qe, re_, rc)
        for r in sn:
            w = self.writer.get(r)
            if w:
                self.qwait(qe, w[0], w[1])
        mw = self.max_waited.get(sem, 0)
        if mw > self.qsynced.get(sem, 0):
            tgt = self.dmacnt.get(sem, 0)
            self.qsynced[sem] = tgt
            self.dma_bound.setdefault(sem, set()).add(tgt)
            lw = self.lastwait[qe]
            if lw.get(sem, -1) < tgt:
                lw[sem] = tgt
                self.ops[qe].append(lambda eng, s=sem, v=tgt: eng.wait_ge(self.sems[s], v))
        self.dmacnt.setdefault(sem, 0)
        self.dmacnt[sem] += 16
        v = self.dmacnt[sem]
        self.ops[qe].append(
            lambda eng, d=dst_ap, s=src_ap, sm=sem, sl=slow: eng.dma_start(
                out=d, in_=s, allow_slow_non_contiguous=sl).then_inc(self.sems[sm], 16))
        for w_ in dn:
            self.writer[w_] = (sem, v)
            self.readers[w_] = {}
        for r in sn:
            self.readers.setdefault(r, {})[sem] = v
        return v


    def mark(self, e):
        if self.cnt[e] > 0:
            return self.cnt[e]
        self.cnt[e] += 1
        c = self.cnt[e]
        cell = [False]
        self.inc_cell[e][c] = cell
        def run(eng, s=e, _cell=cell):
            ins = eng.nop()
            if _cell[0]:
                ins.then_inc(self.sems[s], 1)
            return ins
        self.ops[e].append(run)
        return c

    def finalize(self):
        for sem, total in self.dmacnt.items():
            bounds = sorted(self.dma_bound.get(sem, set()) | {total})
            res = {}
            mw = self.max_waited.get(sem, 0)
            import bisect
            for v in range(16, mw + 16, 16):
                i = bisect.bisect_left(bounds, v)
                res[v] = bounds[i] if i < len(bounds) else total
            self.dma_resolved[sem] = res
        for e in self.CE:
            acc = 0
            res = {}
            need = self.need_inc[e]
            for c in range(1, self.cnt[e] + 1):
                if c in need:
                    self.inc_cell[e][c][0] = True
                    acc += 1
                res[c] = acc
            self.resolved[e] = res

    def barrier(self):
        vals = {}
        for e in self.CE:
            vals[e] = self.mark(e)
        for e in self.CE:
            for o in self.CE:
                if o != e:
                    self.wait(e, o, vals[o])
        for o in self.CE:
            self.qwait("S", o, vals[o])
        for sem, tot in list(self.dmacnt.items()):
            if tot > self.qsynced.get(sem, 0):
                self.qsynced[sem] = tot
                self.dma_bound.setdefault(sem, set()).add(tot)
                lw = self.lastwait["S"]
                if lw.get(sem, -1) < tot:
                    lw[sem] = tot
                    self.ops["S"].append(lambda eng, s=sem, v=tot: eng.wait_ge(self.sems[s], v))

    def mm(self, out, lhsT, rhs, start=True, stop=True, inc=True):
        out = self.ap(out); lhsT = self.ap(lhsT); rhs = self.ap(rhs)
        self.op("T", lambda t, o=out, l=lhsT, r=rhs, st=start, sp=stop:
                t.matmul(out=o, lhsT=l, rhs=r, start=st, stop=sp),
                rdaps=(lhsT, rhs), wraps=(out,), inc=inc)

    def trans(self, out, in_, ident, inc=True):
        out = self.ap(out); in_ = self.ap(in_); ident = self.ap(ident)
        self.op("T", lambda t, o=out, i=in_, d=ident: t.transpose(out=o, in_=i, identity=d),
                rdaps=(in_, ident), wraps=(out,), inc=inc)

    def tt(self, e, out, a, b, alu, inc=True):
        out = self.ap(out); a = self.ap(a); b = self.ap(b)
        self.op(e, lambda g, o=out, x=a, y=b, p=alu: g.tensor_tensor(out=o, in0=x, in1=y, op=p),
                rdaps=(a, b), wraps=(out,), inc=inc)

    def ts(self, e, out, a, s1, op0, s2=None, op1=None, inc=True):
        out = self.ap(out); a = self.ap(a); s1 = self.ap(s1); s2 = self.ap(s2)
        def f(g, o=out, x=a, _s1=s1, _s2=s2, _o0=op0, _o1=op1):
            if _o1 is None:
                return g.tensor_scalar(out=o, in0=x, scalar1=_s1, scalar2=None, op0=_o0)
            return g.tensor_scalar(out=o, in0=x, scalar1=_s1, scalar2=_s2, op0=_o0, op1=_o1)
        rd = [a] + [z for z in (s1, s2) if hasattr(z, "tensor")]
        self.op(e, f, rdaps=rd, wraps=(out,), inc=inc)

    def stt(self, out, a, scalar, b, op0, op1, inc=True, e="V"):
        out = self.ap(out); a = self.ap(a); scalar = self.ap(scalar); b = self.ap(b)
        rd = [a, b] + ([scalar] if hasattr(scalar, "tensor") else [])
        self.op(e, lambda g, o=out, x=a, s=scalar, y=b, p0=op0, p1=op1:
                g.scalar_tensor_tensor(out=o, in0=x, scalar=s, in1=y, op0=p0, op1=p1),
                rdaps=rd, wraps=(out,), inc=inc)

    def red(self, out, in_, alu, absval=False, inc=True):
        out = self.ap(out); in_ = self.ap(in_)
        self.op("V", lambda g, o=out, i=in_, p=alu, av=absval:
                g.tensor_reduce(out=o, in_=i, axis=AX.X, op=p, apply_absolute_value=av),
                rdaps=(in_,), wraps=(out,), inc=inc)

    def act(self, out, in_, func, bias=None, scale=None, inc=True, accum=None):
        out = self.ap(out); in_ = self.ap(in_); bias = self.ap(bias); scale = self.ap(scale); accum = self.ap(accum)
        def f(g, o=out, i=in_, fn=func, b=bias, s=scale, ac=accum):
            kw = {}
            if b is not None:
                kw["bias"] = b
            if s is not None:
                kw["scale"] = s
            if ac is not None:
                kw["accum_out"] = ac
            return g.activation(out=o, in_=i, func=fn, **kw)
        rd = [in_] + [z for z in (bias, scale) if hasattr(z, "tensor")]
        self.op("A", f, rdaps=rd, wraps=(out,) + ((accum,) if accum is not None else ()), inc=inc)

    def copy(self, e, out, in_, inc=True):
        out = self.ap(out); in_ = self.ap(in_)
        if e == "A":
            self.act(out, in_, AF.Identity, inc=inc)
        else:
            self.op(e, lambda g, o=out, i=in_: g.tensor_copy(out=o, in_=i),
                    rdaps=(in_,), wraps=(out,), inc=inc)

    def recip(self, out, in_, inc=True):
        out = self.ap(out); in_ = self.ap(in_)
        self.op("V", lambda g, o=out, i=in_: g.reciprocal(out=o, in_=i),
                rdaps=(in_,), wraps=(out,), inc=inc)

    def memset(self, e, ap, val, inc=True):
        ap = self.ap(ap)
        self.op(e, lambda g, a=ap, v=val: g.memset(a, v), wraps=(ap,), inc=inc)



def _bf(x):
    return np.ascontiguousarray(x, dtype=BFNP)


def _f32(x):
    return np.ascontiguousarray(x, dtype=np.float32)


def prep_core_inputs(I, core):
    d = {}
    d["xin"] = _f32(I["L0_attn_x"][2 * core:2 * core + 2])
    d["bcombT"] = _f32(np.concatenate([I["L0_attn_B_re"].T, I["L0_attn_B_im"].T], axis=1))
    d["creT"] = _f32(I["L0_attn_C_re"].T)
    d["cimT"] = _f32(I["L0_attn_C_im"].T)
    d["dT"] = _f32(I["L0_attn_D"].T)
    d["nu_row"] = _f32(I["L0_attn_nu_log"][None, :])
    d["th_row"] = _f32(I["L0_attn_theta_log"][None, :])
    d["gam_row"] = _f32(I["L0_attn_gamma_log"][None, :])
    d["wg_nat"] = _f32(I["L0_ffn_w_gate"])
    d["wu_nat"] = _f32(I["L0_ffn_w_up"])
    d["wd_nat"] = _f32(I["L0_ffn_w_down"])
    d["wq_n"] = _f32(I["L1_attn_Wq"])
    d["wk_n"] = _f32(I["L1_attn_Wk"])
    d["wv_n"] = _f32(I["L1_attn_Wv"])
    d["wb_n"] = _f32(I["L1_attn_Wb"])
    def col_fm(w):
        K = w.shape[1]
        return _f32(w[:, ::-1].reshape(8, 128, K).transpose(1, 0, 2))
    d["convw"] = np.ascontiguousarray(np.concatenate(
        [col_fm(I["L1_attn_conv_w_q"]), col_fm(I["L1_attn_conv_w_k"]),
         col_fm(I["L1_attn_conv_w_v"])], axis=2))
    d["firw"] = np.ascontiguousarray(np.concatenate(
        [col_fm(I["L1_attn_fir_short_w"].reshape(1024, 5)),
         col_fm(I["L1_attn_fir_long_w"].reshape(1024, 64))], axis=2))
    d["w1a"] = _f32(I["L1_attn_mlp_w1"][:1024])
    d["w1b"] = _f32(I["L1_attn_mlp_w1"][1024:])
    d["b1col"] = _f32(I["L1_attn_mlp_b1"].reshape(8, 128).T)
    d["w2_n"] = _f32(I["L1_attn_mlp_w2"])
    d["b2col"] = _f32(I["L1_attn_mlp_b2"][:, None])
    d["ltcol"] = _f32(np.broadcast_to(I["L1_attn_logit_temp"].reshape(1, 1), (4, 1)))
    d["crl_rep"] = _f32(np.broadcast_to(I["L1_attn_conv_res_logit"][None, :], (128, 4)))
    d["rmsw_tile"] = _f32(np.broadcast_to(np.tile(np.asarray(I["L1_attn_rms_w"]), 4)[None, :], (128, 1024)))
    d["wo_n"] = _f32(I["L1_attn_Wo"])
    d["wfc_nat"] = _f32(I["L1_ffn_w_fc"])
    d["wfcT"] = _f32(np.asarray(I["L1_ffn_w_fc"]).T)
    d["wpj_nat"] = _f32(I["L1_ffn_w_proj"])
    d["wpjT"] = _f32(np.asarray(I["L1_ffn_w_proj"]).T)
    d["uT"] = _f32(np.asarray(I["L1_ffn_u"]).T)
    d["fb_n"] = _f32(I["L1_ffn_fast_b"])
    d["v_n"] = _f32(I["L1_ffn_v"])
    d["gate_rep"] = _f32(np.broadcast_to(np.asarray(I["L1_ffn_gate"]).reshape(1, 1), (128, 1)))
    d["identF"] = _f32(np.eye(128))
    d["identB"] = _bf(np.eye(128))
    d["i4"] = _bf(np.tile(np.eye(128), (1, 4)))
    jj, uu = np.mgrid[0:128, 0:128]
    d["tri"] = _bf((jj <= uu).astype(np.float32))
    d["masksl_neg"] = _bf(-(uu < jj).astype(np.float32))
    d["maskui"] = _bf((uu >= jj).astype(np.float32))
    d["ramp"] = _f32(np.arange(128)[None, :])
    d["ones_row"] = _bf(np.ones((1, 128)))
    d["ones_f"] = _f32(np.ones((1, 128)))
    d["ones_col_f"] = _f32(np.ones((128, 1)))
    d["ramp_col"] = _f32(np.arange(128, dtype=np.float32)[:, None])
    sel = np.zeros((4, 512), np.float32)
    for h in range(4):
        sel[h, h * 128:(h + 1) * 128] = 1.0
    d["sel4"] = _bf(sel)
    ind = np.zeros((128, 8, 4), np.float32)
    for c in range(8):
        ind[:, c, c // 2] = 1.0
    d["ind_h"] = _bf(ind)
    return d


def build_program(layers=(0, 1), ntiles=None, nseqs=None):
    if ntiles is None:
        ntiles = NTILES
    if nseqs is None:
        nseqs = SEQ_PER_CORE
    nc = bass.Bass("TRN2", target_bir_lowering=False, debug=False)
    p = Prog()
    from contextlib import ExitStack
    es = ExitStack()

    def din(name, shape, dt=F32):
        return nc.dram_tensor(name, shape, dt, kind="ExternalInput").ap()
    xin = din("xin", [SEQ_PER_CORE, L, D])
    yout = nc.dram_tensor("yout", [SEQ_PER_CORE, L, D], F32, kind="ExternalOutput").ap()
    bcombT = din("bcombT", [D, 1024]); creT = din("creT", [512, D]); cimT = din("cimT", [512, D])
    dT = din("dT", [D, D])
    nu_row = din("nu_row", [1, 512]); th_row = din("th_row", [1, 512]); gam_row = din("gam_row", [1, 512])
    wg_nat = din("wg_nat", [NI, D]); wu_nat = din("wu_nat", [NI, D]); wd_nat = din("wd_nat", [D, NI])
    wq_n = din("wq_n", [D, D]); wk_n = din("wk_n", [D, D]); wv_n = din("wv_n", [D, D]); wb_n = din("wb_n", [D, 4])
    convw = din("convw", [128, 8, 12]); firw = din("firw", [128, 8, 69])
    w1a = din("w1a", [D, D]); w1b_in = din("w1b", [16, D]); b1col = din("b1col", [128, 8])
    w2_n = din("w2_n", [D, 4]); b2col = din("b2col", [4, 1]); ltcol = din("ltcol", [4, 1])
    crl_rep = din("crl_rep", [128, 4]); rmsw_in = din("rmsw_tile", [128, D])
    wo_n = din("wo_n", [D, D])
    wfc_nat = din("wfc_nat", [FFI, D]); wfcT = din("wfcT", [D, FFI])
    wpj_nat = din("wpj_nat", [D, FFI]); wpjT = din("wpjT", [FFI, D])
    uT = din("uT", [16, FFI]); fb_n = din("fb_n", [16, 16]); v_n = din("v_n", [16, D])
    gate_rep = din("gate_rep", [128, 1])
    identF = din("identF", [128, 128]); identB = din("identB", [128, 128], BF16)
    i4 = din("i4", [128, 512], BF16); tri_in = din("tri", [128, 128], BF16)
    masksl_in = din("masksl_neg", [128, 128], BF16); maskui_in = din("maskui", [128, 128], BF16)
    ramp_in = din("ramp", [1, 128]); ones_row_in = din("ones_row", [1, 128], BF16)
    ones_f_in = din("ones_f", [1, 128])
    ones_col_in = din("ones_col_f", [128, 1])
    rampc_in = din("ramp_col", [128, 1])
    sel4_in = din("sel4", [4, 512], BF16)
    ind_in = din("ind_h", [128, 8, 4], BF16)

    def scr(name, shape):
        return nc.dram_tensor(name, shape, BF16, kind="Internal").ap()
    s_bcomb = scr("s_bcomb", [8, 128, 1024])
    s_ccomb = scr("s_ccomb", [8, 128, 1024])
    s_dT = scr("s_dT", [8, 128, 1024])
    s_wgT = scr("s_wgT", [8, 128, NI]); s_wuT = scr("s_wuT", [8, 128, NI])
    s_wdT = scr("s_wdT", [22, 128, 1024])
    s_wq = scr("s_wq", [8, 128, 1024]); s_wk = scr("s_wk", [8, 128, 1024]); s_wv = scr("s_wv", [8, 128, 1024])
    s_wb = scr("s_wb", [8, 128, 4])
    s_w1a = scr("s_w1a", [8, 128, 1024])
    s_wo = scr("s_wo", [8, 128, 1024])
    s_weffT = scr("s_weffT", [8, 128, FFI])
    s_wpjT = scr("s_wpjT", [32, 128, 1024])
    s_row = nc.dram_tensor("s_row", [1, 512], F32, kind="Internal").ap()

    for nm in ("T", "V", "A", "G", "dW", "dX", "dY", "dP", "dG"):
        p.sems[nm] = es.enter_context(nc.semaphore(nm))

    def sb(name, shape, dt=F32):
        return es.enter_context(nc.sbuf_tensor(name, shape, dt))

    ident_f = sb("ident_f", [128, 128]); ident_b = sb("ident_b", [128, 128], BF16)
    i4_sb = sb("i4_sb", [128, 512], BF16); tri_sb = sb("tri_sb", [128, 128], BF16)
    masksl = sb("masksl", [128, 128], BF16); maskui = sb("maskui_s", [128, 128], BF16)
    ramp_sb = sb("ramp_sb", [1, 128]); ones_sb = sb("ones_sb", [1, 128], BF16)
    ones_f = sb("ones_f_s", [1, 128])
    ones_col_f = sb("ones_col_f_s", [128, 1])
    ramp_col = sb("ramp_col_s", [128, 1])
    sel4 = sb("sel4_s", [4, 512], BF16)
    ind_sb = sb("ind_sb", [128, 8, 4], BF16)
    convw_sb = sb("convw_sb", [128, 8, 12]); firw_sb = sb("firw_sb", [128, 8, 69])
    rmsw_sb = sb("rmsw_sb", [128, D], BF16)
    b1_sb = sb("b1_sb", [128, 8]); w1b_sb = sb("w1b_sb", [16, D], BF16)
    w2_sb = sb("w2_sb", [128, 8, 4], BF16)
    b2_sb = sb("b2_sb", [4, 1]); rtemp_sb = sb("rtemp_sb", [4, 1]); b2rt_sb = sb("b2rt_sb", [4, 1])
    crl_sb = sb("crl_sb", [128, 4]); gate_sb = sb("gate_sb", [128, 1])
    lm_re = sb("lm_re", [128, 512], BF16); lm_im = sb("lm_im", [128, 512], BF16)
    lp_re = sb("lp_re", [128, 512], BF16); lp_im = sb("lp_im", [128, 512], BF16)
    lam_col = sb("lam_col", [128, 8])
    x_t = sb("x_t", [128, TC, 1024])
    xn_fm = sb("xn_fm", [128, 8, TILE], BF16)
    fm2 = sb("fm2", [128, 8, TILE], BF16)
    q_fm = sb("q_fm", [128, 8, TILE], BF16)
    k_fm = sb("k_fm", [128, 8, TILE], BF16)
    kb_fm = sb("kb_fm", [128, 8, TILE], BF16)
    vconv = sb("vconv", [128, 8, 64 + TILE], BF16)
    scan_carry = sb("scan_carry", [128, 8])
    lamS = sb("lamS", [128, 8])
    Sst = sb("Sst", [128, 8, DK])
    qtail = sb("qtail", [128, 8, 3], BF16); ktail = sb("ktail", [128, 8, 3], BF16)
    vtail = sb("vtail", [128, 8, 3], BF16)
    ls_tm = sb("ls_tm", [128, TC, 1024], BF16)
    ll_tm = sb("ll_tm", [128, TC, 1024], BF16)
    beta_f = sb("beta_f", [4, TILE], BF16)
    beta_tm = sb("beta_tm", [128, TC, 4])
    rsk_tm = sb("rsk_tm", [128, TC, 4])

    PS = [es.enter_context(nc.psum_tensor(f"ps{i}", [128, 512], F32)) for i in range(8)]

    def bank(i, lo=0, n=512):
        return PS[i][:, lo:lo + n]

    def bankb(i, lo=0, n=1024):
        return PS[i][:].bitcast(BF16)[:, lo:lo + n]

    W = {"v": 0}

    def ld_wait(e, v, sem="dW"):
        p.wait(e, sem, v)

    def quant_fp4(dst, src, N, tm):
        nb = N // 16
        qsc = tm["qsc"][:, :nb]; qrs = tm["qrs"][:, :nb]; qscb = tm["qscb"][:, :nb]
        xs = tm["qf1"][:, :N]; t = tm["qf2"][:, :N]
        A = tm["qb1"][:, :N]; Bb = tm["qb2"][:, :N]; C = tm["qb3"][:, :N]
        Dd = tm["qb4"][:, :N]; E = tm["qb5"][:, :N]
        r3 = lambda ap: ap.rearrange("p (a b) -> p a b", b=16)
        p.red(qsc, r3(src), OP.max, absval=True)
        p.ts("V", qsc, qsc, 1.0 / 6.0, OP.mult, 1e-12, OP.max)
        p.recip(qrs, qsc)
        p.copy("A", qscb, qsc)
        p.tt("V", r3(xs), r3(src), qrs.unsqueeze(2).broadcast_to((128, nb, 16)), OP.mult)
        p.ts("V", t, xs, -1.0, OP.mult)
        p.tt("V", t, t, xs, OP.max)
        p.ts("V", A, t, 0.25, OP.is_gt)
        p.stt(A, t, 0.75, A, OP.is_gt, OP.add)
        p.stt(A, t, 1.25, A, OP.is_gt, OP.add)
        p.stt(A, t, 1.75, A, OP.is_gt, OP.add)
        p.ts("V", Bb, t, 2.5, OP.is_gt)
        p.stt(Bb, t, 3.5, Bb, OP.is_gt, OP.add)
        p.stt(Bb, t, 5.0, Bb, OP.is_gt, OP.add)
        p.stt(Bb, t, 5.0, Bb, OP.is_gt, OP.add)
        p.stt(E, A, 0.5, Bb, OP.mult, OP.add)
        p.ts("V", A, xs, 0.0, OP.is_ge, 2.0, OP.mult)
        p.tt("V", r3(Bb), r3(E), qscb.unsqueeze(2).broadcast_to((128, nb, 16)), OP.mult)
        p.stt(dst, A, -1.0, Bb, OP.add, OP.mult)

    def range_sin(dst, m_ap, tm, phase_quarter=False):
        ti = tm["ti"]; f = tm["tf1"]; g = tm["tf2"]
        src = m_ap
        if phase_quarter:
            p.ts("V", f, src, 0.25, OP.add)
            src = f
        p.ts("V", ti, src, 1.0, OP.mult)
        p.tt("V", f, src, ti, OP.subtract)
        p.ts("V", g, f, 0.5, OP.is_ge)
        p.tt("V", f, f, g, OP.subtract)
        p.act(dst, f, AF.Sin, scale=TWO_PI)

    cl = []
    for dst, src in ((ident_f, identF), (ident_b, identB), (i4_sb, i4), (tri_sb, tri_in),
                     (masksl, masksl_in), (maskui, maskui_in), (ramp_sb, ramp_in),
                     (ones_sb, ones_row_in), (ones_f, ones_f_in), (ones_col_f, ones_col_in), (ramp_col, rampc_in), (sel4, sel4_in), (ind_sb[:].rearrange("p a b -> p (a b)"), ind_in.rearrange("p a b -> p (a b)")),
                     (convw_sb[:].rearrange("p a b -> p (a b)"), convw.rearrange("p a b -> p (a b)")),
                     (firw_sb[:].rearrange("p a b -> p (a b)"), firw.rearrange("p a b -> p (a b)")),
                     (b1_sb, b1col), (b2_sb, b2col),
                     (crl_sb[:], crl_rep), (gate_sb, gate_rep)):
        cl.append(p.load(dst if hasattr(dst, "tensor") else dst[:], src, "dP"))
    p.load(rmsw_sb[:], rmsw_in, "dG", qe="G")
    v_const = p.dmacnt["dP"]
    for e in ("T", "V", "A"):
        p.wait(e, "dP", v_const)

    with ExitStack() as _ph:
        _en = _ph.enter_context
        prow_t = _en(nc.sbuf_tensor("prow", [1, 512], F32))
        prow2_t = _en(nc.sbuf_tensor("prow2", [1, 512], F32))
        prow3_t = _en(nc.sbuf_tensor("prow3", [1, 512], F32))
        prow4_t = _en(nc.sbuf_tensor("prow4", [1, 512], F32))
        pti_t = _en(nc.sbuf_tensor("pti", [128, 512], I32))
        ptf1_t = _en(nc.sbuf_tensor("ptf1", [128, 512], F32))
        ptf2_t = _en(nc.sbuf_tensor("ptf2", [128, 512], F32))
        pexp_t = _en(nc.sbuf_tensor("pexp", [128, 512], F32))
        psin_t = _en(nc.sbuf_tensor("psin", [128, 512], F32))
        pcos_t = _en(nc.sbuf_tensor("pcos", [128, 512], F32))
        pgc_t = _en(nc.sbuf_tensor("pgc", [128, 8], F32))
        prow = prow_t[:]; prow2 = prow2_t[:]; prow3 = prow3_t[:]; prow4 = prow4_t[:]
        tmr = {"ti": pti_t[:], "tf1": ptf1_t[:], "tf2": ptf2_t[:]}
        eb = pexp_t; pb2 = psin_t; gb = pcos_t
        v1 = p.load(eb[:], nu_row.partition_broadcast(128).squeeze(1), "dP")
        v2 = p.load(pb2[:], th_row.partition_broadcast(128).squeeze(1), "dP")
        v3 = p.load(gb[:], gam_row.partition_broadcast(128).squeeze(1), "dP")
        for e in ("T", "V", "A"):
            p.wait(e, "dP", max(v1, v2, v3))
        p.act(eb[:], eb[:], AF.Exp)
        p.act(pb2[:], pb2[:], AF.Exp)
        p.act(gb[:], gb[:], AF.Exp)
        tmq = {"ti": pti_t[:, 0:128], "tf1": ptf1_t[:, 0:128], "tf2": ptf2_t[:, 0:128]}
        lptmp_re = prow_t; lptmp_im = prow2_t
        for cc in range(4):
            sl = slice(cc * 128, (cc + 1) * 128)
            m1 = ptf1_t[:, 128:256]; m2 = ptf1_t[:, 256:384]
            sinb = ptf2_t[:, 128:256]; cosb = ptf2_t[:, 256:384]; exb = ptf2_t[:, 384:512]
            p.ts("V", m1, eb[:, sl], ramp_col[:], OP.mult)
            p.ts("V", m2, pb2[:, sl], ramp_col[:], OP.mult, 1.0 / TWO_PI, OP.mult)
            p.act(exb, m1, AF.Exp)
            range_sin(sinb, m2, tmq)
            range_sin(cosb, m2, tmq, phase_quarter=True)
            p.tt("V", m1, exb, cosb, OP.mult)
            p.tt("V", lm_re[:, sl], m1, gb[:, sl], OP.mult)
            p.stt(m1, sinb, -1.0, exb, OP.mult, OP.mult)
            p.tt("V", lm_im[:, sl], m1, gb[:, sl], OP.mult)
            p.ts("V", m1, eb[:, sl], ramp_col[:], OP.mult)
            p.act(exb, m1, AF.Exp, scale=-1.0)
            p.tt("V", m1, exb, cosb, OP.mult)
            p.tt("V", m2, exb, sinb, OP.mult)
            p.trans(PS[6][:, 0:128], m1, ident_f)
            p.trans(PS[6][:, 128:256], m2, ident_f)
            p.copy("A", lp_re[:, sl], PS[6][:, 0:128])
            p.copy("A", lp_im[:, sl], PS[6][:, 128:256])
            p.copy("V", lam_col[:, cc:cc + 1], lp_re[:, cc * 128 + 1:cc * 128 + 2])
            p.copy("V", lam_col[:, 4 + cc:5 + cc], lp_im[:, cc * 128 + 1:cc * 128 + 2])

    def emit_weight_prologue():
      with ExitStack() as _ph:
        _en = _ph.enter_context
        wpin_t = _en(nc.sbuf_tensor("wpin", [128, 1, FFI], F32))
        wpout_t = _en(nc.sbuf_tensor("wpout", [128, 1, FFI], BF16))
        prqsc_t = _en(nc.sbuf_tensor("prqsc", [128, 1, 512], F32))
        prqf_t = _en(nc.sbuf_tensor("prqf", [128, 2, 704], F32))
        prqb_t = _en(nc.sbuf_tensor("prqb", [128, 5, 704], BF16))
        prrow_t = _en(nc.sbuf_tensor("prrow", [1, 512], F32))
        prbc_t = _en(nc.sbuf_tensor("prbc", [128, FFI], BF16))
        prsc_t = _en(nc.sbuf_tensor("prsc", [128, 32], F32))
        prvf_t = _en(nc.sbuf_tensor("prvf", [128, FFI], BF16))
        rbcq_t = _en(nc.sbuf_tensor("rbcq", [128, FFI], F32))

        tmpq = {"qsc": prqsc_t[:, 0, 0:176], "qrs": prqsc_t[:, 0, 176:352],
                "qscb": prqsc_t[:, 0, 352:440].bitcast(BF16),
                "qf1": prqf_t[:, 0, :], "qf2": prqf_t[:, 1, :],
                "qb1": prqb_t[:, 0, :], "qb2": prqb_t[:, 1, :], "qb3": prqb_t[:, 2, :],
                "qb4": prqb_t[:, 3, :], "qb5": prqb_t[:, 4, :]}

        def cast_stream(dst_scr, src_dram, nch, width, scale_cols=None, dst_off=0):
            for c in range(nch):
                src = src_dram.rearrange("(n q) w -> n q w", q=128)[c]
                p.load(wpin_t[:, 0, 0:width], src, "dP")
                if scale_cols is None:
                    p.act(wpout_t[:, 0, 0:width], wpin_t[:, 0, 0:width], AF.Identity)
                else:
                    p.act(wpout_t[:, 0, 0:width], wpin_t[:, 0, 0:width], AF.Identity,
                          scale=scale_cols[c])
                p.load(dst_scr[dst_off + c], wpout_t[:, 0, 0:width], "dP")

        cast_stream(s_bcomb, bcombT, 8, 1024)
        cast_stream(s_dT, dT, 8, 1024)
        cast_stream(s_ccomb, creT, 4, 1024)
        cast_stream(s_ccomb, cimT, 4, 1024, scale_cols=[-1.0] * 4, dst_off=4)
        cast_stream(s_wq, wq_n, 8, 1024)
        cast_stream(s_wk, wk_n, 8, 1024)
        cast_stream(s_wv, wv_n, 8, 1024)
        cast_stream(s_wb, wb_n, 8, 4)
        cast_stream(s_w1a, w1a, 8, 1024)
        cast_stream(s_wo, wo_n, 8, 1024)
        p.load(wpin_t[0:16, 0, 0:1024], w1b_in, "dP")
        p.copy("V", w1b_sb[:], wpin_t[0:16, 0, 0:1024])
        p.load(wpin_t[:, 0, 1024:1056].rearrange("p (c w) -> p c w", w=4), w2_n.rearrange("(c q) w -> q c w", q=128), "dP")
        p.copy("V", w2_sb[:].rearrange("p a b -> p (a b)"), wpin_t[:, 0, 1024:1056])
        p.load(rtemp_sb[:], ltcol, "dP")
        p.ts("V", b2rt_sb[:], rtemp_sb[:], -1.0, OP.mult)
        p.tt("V", b2rt_sb[:], b2rt_sb[:], rtemp_sb[:], OP.max)
        p.act(b2rt_sb[:], b2rt_sb[:], AF.Exp, scale=-1.0)
        p.act(b2rt_sb[:], b2rt_sb[:], AF.Ln, bias=1.0)
        p.ts("V", rtemp_sb[:], rtemp_sb[:], 0.0, OP.max)
        p.tt("V", rtemp_sb[:], rtemp_sb[:], b2rt_sb[:], OP.add)
        p.ts("V", rtemp_sb[:], rtemp_sb[:], 1e-6, OP.add)
        p.recip(rtemp_sb[:], rtemp_sb[:])
        p.tt("V", b2rt_sb[:], b2_sb[:], rtemp_sb[:], OP.mult)
        p.act(crl_sb[:], crl_sb[:], AF.Sigmoid)

        def fp4_stream(dst_scr, src_dram, nch, width, kchunks):
            for c in range(nch):
                src = src_dram.rearrange("(n q) w -> n q w", q=128)[c]
                p.load(wpin_t[:, 0, 0:width], src, "dP")
                wq = 704 if width % 704 == 0 else 512
                for q_ in range(width // wq):
                    quant_fp4(wpout_t[:, 0, q_ * wq:(q_ + 1) * wq],
                              wpin_t[:, 0, q_ * wq:(q_ + 1) * wq], wq, tmpq)
                for kc in range(kchunks):
                    pb = 6 + (kc % 2)
                    p.trans(bankb(pb, 0, 128), wpout_t[:, 0, kc * 128:(kc + 1) * 128], ident_b)
                    p.copy("A", prvf_t[:, kc * 128:(kc + 1) * 128], bankb(pb, 0, 128))
                for kc in range(kchunks):
                    p.load(dst_scr[kc][:, c * 128:(c + 1) * 128],
                           prvf_t[:, kc * 128:(kc + 1) * 128], "dP")

        fp4_stream(s_wgT, wg_nat, 22, 1024, 8)
        fp4_stream(s_wuT, wu_nat, 22, 1024, 8)
        fp4_stream(s_wdT, wd_nat, 8, NI, 22)



        def tern_stream(dst_scr, srcT, kchunks, N, kin, extra=None):
            for sl in range(N // 512):
                for kc in range(kin // 128):
                    p.load(wpin_t[:, 0, 0:512],
                           srcT.rearrange("(n q) w -> n q w", q=128)[kc][:, sl * 512:(sl + 1) * 512], "dP")
                    p.ts("V", prqf_t[:, 0, 0:512], wpin_t[:, 0, 0:512], -1.0, OP.mult)
                    p.tt("V", prqf_t[:, 0, 0:512], prqf_t[:, 0, 0:512], wpin_t[:, 0, 0:512], OP.max)
                    p.mm(PS[7][0:1, 0:512], ones_col_f[:], prqf_t[:, 0, 0:512],
                         start=(kc == 0), stop=(kc == kin // 128 - 1))
                p.ts("V", prrow_t[0:1, 0:512], PS[7][0:1, 0:512], 1.0 / kin, OP.mult, 1e-5, OP.max)
                p.load(s_row, prrow_t[0:1, 0:512], "dP")
                p.load(prqf_t[:, 0, 0:512], s_row.partition_broadcast(128).squeeze(1), "dP")
                p.copy("V", prbc_t[:, sl * 512:(sl + 1) * 512], prqf_t[:, 0, 0:512])
                p.recip(prrow_t[0:1, 0:512], prrow_t[0:1, 0:512])
                p.load(s_row, prrow_t[0:1, 0:512], "dP")
                p.load(rbcq_t[:, sl * 512:(sl + 1) * 512], s_row.partition_broadcast(128).squeeze(1), "dP")
            for kc in range(kchunks):
                p.load(wpin_t[:, 0, 0:N], srcT.rearrange("(n q) w -> n q w", q=128)[kc], "dP")
                if extra is not None:
                    extra(kc)
                for q_ in range(N // 512):
                    qs = slice(q_ * 512, (q_ + 1) * 512)
                    t = prqf_t[:, 1, 0:512]
                    p.tt("V", t, wpin_t[:, 0, qs], rbcq_t[:, qs], OP.mult)
                    a = prqb_t[:, 0, 0:512]; sg = prqb_t[:, 1, 0:512]; q1 = prqb_t[:, 2, 0:512]
                    af = prqf_t[:, 0, 0:512]
                    p.ts("V", af, t, -1.0, OP.mult)
                    p.tt("V", af, af, t, OP.max)
                    p.ts("V", a, af, 0.5, OP.is_gt)
                    p.ts("V", sg, t, 0.0, OP.is_ge, 2.0, OP.mult)
                    p.stt(q1, sg, -1.0, a, OP.add, OP.mult)
                    p.tt("V", wpout_t[:, 0, qs], q1, prbc_t[:, qs], OP.mult)
                if extra is not None:
                    p.stt(wpout_t[:, 0, 0:N], prvf_t[:, 0:N], gate_sb[:, 0:1],
                          wpout_t[:, 0, 0:N], OP.mult, OP.add)
                p.load(dst_scr[kc], wpout_t[:, 0, 0:N], "dP")

        with ExitStack() as _ph:
            _en = _ph.enter_context
            lru_t = _en(nc.sbuf_tensor("lru", [16, FFI], BF16))
            lrfb_t = _en(nc.sbuf_tensor("lrfb", [16, 16], BF16))
            lrv_t = _en(nc.sbuf_tensor("lrv", [16, 1024], BF16))
            lrfbu_t = _en(nc.sbuf_tensor("lrfbu", [16, FFI], BF16))
            for q_ in range(2):
                p.load(wpin_t[0:16, 0, 0:2048], uT[:, q_ * 2048:(q_ + 1) * 2048], "dP")
                p.copy("V", lru_t[0:16, q_ * 2048:(q_ + 1) * 2048], wpin_t[0:16, 0, 0:2048])
            p.load(wpin_t[0:16, 0, 0:16], fb_n, "dP")
            p.copy("V", lrfb_t[:], wpin_t[0:16, 0, 0:16])
            p.load(wpin_t[0:16, 0, 0:1024], v_n, "dP")
            p.copy("V", lrv_t[:], wpin_t[0:16, 0, 0:1024])
            for sl in range(FFI // 512):
                p.mm(PS[7][0:16, 0:512], lrfb_t[:], lru_t[0:16, sl * 512:(sl + 1) * 512])
                p.copy("A", lrfbu_t[0:16, sl * 512:(sl + 1) * 512], PS[7][0:16, 0:512])

            def make_vfbu(kc):
                for sl in range(FFI // 512):
                    p.mm(bank(6, 0, 512), lrv_t[0:16, kc * 128:(kc + 1) * 128],
                         lrfbu_t[0:16, sl * 512:(sl + 1) * 512])
                    p.copy("A", prvf_t[:, sl * 512:(sl + 1) * 512], bank(6, 0, 512))

            tern_stream(s_weffT, wfcT, 8, FFI, 1024, extra=make_vfbu)
            tern_stream(s_wpjT, wpjT, 32, 1024, FFI)

    p.barrier()
    emit_weight_prologue()
    p.barrier()

    def ttr(out_dump, in0, in1, accum, op0=OP.mult, op1=OP.add):
        p.tt("V", out_dump, in0, in1, op0)
        p.red(accum, out_dump, op1)

    OUTB = {0: (0, 1), 1: (2, 3), 2: (0, 1), 3: (2, 3)}

    def emit_ln(xsrc, xnf, xnb, lnc, dump):
        p.red(lnc[:, 0:1], xsrc, OP.add)
        ttr(dump, xsrc, xsrc, lnc[:, 1:2])
        p.ts("V", lnc[:, 2:3], lnc[:, 0:1], 1.0 / 1024, OP.mult)
        p.ts("V", lnc[:, 3:4], lnc[:, 1:2], 1.0 / 1024, OP.mult)
        p.tt("V", lnc[:, 4:5], lnc[:, 2:3], lnc[:, 2:3], OP.mult)
        p.tt("V", lnc[:, 4:5], lnc[:, 3:4], lnc[:, 4:5], OP.subtract)
        p.ts("V", lnc[:, 4:5], lnc[:, 4:5], 1e-5, OP.add)
        p.recip(lnc[:, 5:6], lnc[:, 4:5])
        p.act(lnc[:, 6:7], lnc[:, 5:6], AF.Sqrt)
        p.stt(lnc[:, 7:8], lnc[:, 2:3], -1.0, lnc[:, 6:7], OP.mult, OP.mult)
        p.ts("V", xnf, xsrc, lnc[:, 6:7], OP.mult, lnc[:, 7:8], OP.add)
        if xnb is not None:
            p.copy("A", xnb, xnf)

    def emit_trans_to_fm(src_tm, dst_fm_chunks, tc, nk, pb0=6, f32=False, kc0=0):
        for g0 in range(0, nk, 4):
            gn = min(4, nk - g0)
            pb = pb0 + (g0 // 4) % 2
            for j in range(gn):
                if f32:
                    p.trans(PS[pb][:, j * 128:(j + 1) * 128],
                            src_tm[:, (g0 + j) * 128:(g0 + j + 1) * 128], ident_f)
                else:
                    p.trans(PS[pb][:].bitcast(BF16)[:, j * 256:j * 256 + 128],
                            src_tm[:, (g0 + j) * 128:(g0 + j + 1) * 128], ident_b)
            if f32:
                p.copy("A", dst_fm_chunks[:, kc0 + g0:kc0 + g0 + gn, tc * 128:(tc + 1) * 128],
                       PS[pb][:].rearrange("p (a b) -> p a b", b=128)[:, 0:gn, :])
            else:
                p.copy("A", dst_fm_chunks[:, kc0 + g0:kc0 + g0 + gn, tc * 128:(tc + 1) * 128],
                       PS[pb][:].bitcast(BF16).rearrange("p (a b) -> p a b", b=256)[:, 0:gn, 0:128])

    def emit_tile(seq, ti, first):
        t0 = ti * TILE
        if first:
            p.memset("V", scan_carry[:], 0.0)
            p.memset("V", Sst[:].rearrange("p a b -> p (a b)"), 0.0)
            p.memset("V", qtail[:].rearrange("p a b -> p (a b)"), 0.0)
            p.memset("V", ktail[:].rearrange("p a b -> p (a b)"), 0.0)
            p.memset("V", vtail[:].rearrange("p a b -> p (a b)"), 0.0)
            p.memset("V", vconv[:, :, 0:64], 0.0)
        p.load(x_t[:], xin[seq, t0:t0 + TILE, :].rearrange("(c q) f -> q c f", q=128), "dX")
        if 0 not in layers:
            p.load(yout[seq, t0:t0 + TILE, :].rearrange("(c q) f -> q c f", q=128), x_t[:], "dY")
            return

        with ExitStack() as _ph:
            _en = _ph.enter_context
            wbc_t = _en(nc.sbuf_tensor(f"wbc_{seq}_{ti}", [128, 8, 1024], BF16))
            wcc_t = _en(nc.sbuf_tensor(f"wcc_{seq}_{ti}", [128, 8, 1024], BF16))
            wdt_t = _en(nc.sbuf_tensor(f"wdt_{seq}_{ti}", [128, 8, 1024], BF16))
            wgs_t = _en(nc.sbuf_tensor(f"wgs_{seq}_{ti}", [128, 8, 512], BF16))
            wus_t = _en(nc.sbuf_tensor(f"wus_{seq}_{ti}", [128, 8, 512], BF16))
            wds_t = _en(nc.sbuf_tensor(f"wds_{seq}_{ti}", [128, 4, 1024], BF16))
            xnf_t = _en(nc.sbuf_tensor(f"xnf0_{seq}_{ti}", [128, 1024], F32))
            xqb_t = _en(nc.sbuf_tensor(f"xqb0_{seq}_{ti}", [128, 512], BF16))
            lnd_t = _en(nc.sbuf_tensor(f"lnd0_{seq}_{ti}", [128, 1024], F32))
            lnc_t = _en(nc.sbuf_tensor(f"lnc0_{seq}_{ti}", [128, 16], F32))
            qf1_t = _en(nc.sbuf_tensor(f"qf1_{seq}_{ti}", [128, 512], F32))
            qf2_t = _en(nc.sbuf_tensor(f"qf2_{seq}_{ti}", [128, 512], F32))
            qsc3_t = _en(nc.sbuf_tensor(f"qsc3_{seq}_{ti}", [128, 3, 64], F32))
            qscb3_t = _en(nc.sbuf_tensor(f"qscb3_{seq}_{ti}", [128, 64], BF16))
            qb5_t = _en(nc.sbuf_tensor(f"qb5_{seq}_{ti}", [128, 5, 512], BF16))
            hf_t = _en(nc.sbuf_tensor(f"hf0_{seq}_{ti}", [128, 512], F32))
            hqf_t = _en(nc.sbuf_tensor(f"hqf0_{seq}_{ti}", [128, 4, 128], BF16))
            qsc3b_t = _en(nc.sbuf_tensor(f"qsc3b_{seq}_{ti}", [128, 2, 64], F32))
            qscb3b_t = _en(nc.sbuf_tensor(f"qscb3b_{seq}_{ti}", [128, 64], BF16))
            qb5b_t = _en(nc.sbuf_tensor(f"qb5b_{seq}_{ti}", [128, 5, 512], BF16))
            hfb_t = _en(nc.sbuf_tensor(f"hfb0_{seq}_{ti}", [128, 512], F32))
            ctm_t = qb5b_t.ap().rearrange("p a b -> p (a b)")[:, 0:1024]
            qtm = {"qsc": qsc3_t[:, 0, :], "qrs": qsc3_t[:, 1, :], "qscb": qscb3_t[:],
                   "qf1": qf1_t[:], "qf2": qf2_t[:],
                   "qb1": qb5_t[:, 0, :], "qb2": qb5_t[:, 1, :], "qb3": qb5_t[:, 2, :],
                   "qb4": qb5_t[:, 3, :], "qb5": qb5_t[:, 4, :]}
            qtmB = {"qsc": qsc3b_t[:, 0, :], "qrs": qsc3b_t[:, 1, :], "qscb": qscb3b_t[:],
                    "qf1": lnd_t.ap()[:, 0:512], "qf2": lnd_t.ap()[:, 512:1024],
                    "qb1": qb5b_t[:, 0, :], "qb2": qb5b_t[:, 1, :], "qb3": qb5b_t[:, 2, :],
                    "qb4": qb5b_t[:, 3, :], "qb5": qb5b_t[:, 4, :]}
            p.load(wbc_t[:], s_bcomb.rearrange("c q w -> q c w"), "dW")
            p.load(wcc_t[:], s_ccomb.rearrange("c q w -> q c w"), "dW")
            p.load(wdt_t[:], s_dT.rearrange("c q w -> q c w"), "dW")
            for tc in range(TC):
                emit_ln(x_t[:, tc, :], xnf_t[:], None, lnc_t[:], lnd_t[:])
                emit_trans_to_fm(xnf_t[:], xn_fm[:], tc, 8, f32=True)
                for hh in range(2):
                    quant_fp4(xqb_t[:], xnf_t[:, hh * 512:(hh + 1) * 512], 512, qtm)
                    emit_trans_to_fm(xqb_t[:], fm2[:], tc, 4, kc0=4 * hh)
            for s in range(NSLICES):
                w = min(N_SLICE, NI - s * N_SLICE)
                nkc = (w + 127) // 128
                p.load(wgs_t[:, :, 0:w], s_wgT.rearrange("c q w -> q c w")[:, :, s * 512:s * 512 + w], "dG", qe="G")
                p.load(wus_t[:, :, 0:w], s_wuT.rearrange("c q w -> q c w")[:, :, s * 512:s * 512 + w], "dG", qe="G")
                p.load(wds_t[:, 0:nkc, :], s_wdT.rearrange("c q w -> q c w")[:, 4 * s:4 * s + nkc, :], "dG", qe="G")
                def f0_stage1(tc):
                    tsl = slice(tc * 128, (tc + 1) * 128)
                    gb, ub = (4, 5) if tc % 2 == 0 else (6, 7)
                    for kc in range(8):
                        p.mm(bank(gb, 0, w), fm2[:, kc, tsl], wgs_t[:, kc, 0:w],
                             start=(kc == 0), stop=(kc == 7))
                    for kc in range(8):
                        p.mm(bank(ub, 0, w), fm2[:, kc, tsl], wus_t[:, kc, 0:w],
                             start=(kc == 0), stop=(kc == 7))

                def f0_stage2(tc):
                    par = tc % 2
                    gb, ub = (4, 5) if par == 0 else (6, 7)
                    qt = (qtm, qtmB)[par]
                    hfx = (hf_t, hfb_t)[par]
                    qbx = (qb5_t, qb5b_t)[par]
                    hsg = qbx[:, 4, 0:w]
                    p.act(hsg, bank(gb, 0, w), AF.Silu)
                    p.tt("V", hfx[:, 0:w], hsg, bank(ub, 0, w), OP.mult)
                    quant_fp4(qbx[:, 2, 0:w], hfx[:, 0:w], w, qt)

                def f0_stage3(tc):
                    par = tc % 2
                    tb = 2 if par == 0 else 0
                    qbx = (qb5_t, qb5b_t)[par]
                    ob = OUTB[tc]
                    for j in range(nkc):
                        p.trans(PS[tb][:].bitcast(BF16)[:, 0:128], qbx[:, 2, j * 128:(j + 1) * 128], ident_b)
                        p.copy("A", hqf_t[:, j, :], PS[tb][:].bitcast(BF16)[:, 0:128])
                    for j in range(nkc):
                        p.mm(bank(ob[0], 0, 512), hqf_t[:, j, :], wds_t[:, j, 0:512],
                             start=(j == 0), stop=(j == nkc - 1))
                        p.mm(bank(ob[1], 0, 512), hqf_t[:, j, :], wds_t[:, j, 512:1024],
                             start=(j == 0), stop=(j == nkc - 1))
                    p.tt("V", x_t[:, tc, 0:512], x_t[:, tc, 0:512], bank(ob[0], 0, 512), OP.add)
                    p.tt("V", x_t[:, tc, 512:1024], x_t[:, tc, 512:1024], bank(ob[1], 0, 512), OP.add)

                for tc in range(TC):
                    f0_stage1(tc)
                    if tc >= 1:
                        f0_stage2(tc - 1)
                    if tc >= 2:
                        f0_stage3(tc - 2)
                f0_stage2(3)
                f0_stage3(2)
                f0_stage3(3)
            if True:
                for tc in range(TC):
                    tsl = slice(tc * 128, (tc + 1) * 128)
                    ob = OUTB[tc]
                    for kc in range(8):
                        p.mm(bank(6, 0, 512), xn_fm[:, kc, tsl], wbc_t[:, kc, 0:512],
                             start=(kc == 0), stop=(kc == 7))
                        p.mm(bank(7, 0, 512), xn_fm[:, kc, tsl], wbc_t[:, kc, 512:1024],
                             start=(kc == 0), stop=(kc == 7))
                    p.tt("V", qb5_t[:, 0, 0:512], lm_re[:], bank(6, 0, 512), OP.mult)
                    p.tt("V", qb5_t[:, 1, 0:512], lm_im[:], bank(7, 0, 512), OP.mult)
                    p.tt("V", ctm_t[:, 0:512], qb5_t[:, 0, 0:512], qb5_t[:, 1, 0:512], OP.subtract)
                    p.tt("V", qb5_t[:, 0, 0:512], lm_re[:], bank(7, 0, 512), OP.mult)
                    p.tt("V", qb5_t[:, 1, 0:512], lm_im[:], bank(6, 0, 512), OP.mult)
                    p.tt("V", ctm_t[:, 512:1024], qb5_t[:, 0, 0:512], qb5_t[:, 1, 0:512], OP.add)
                    p.tt("V", lnc_t[:, 8:12], lam_col[:, 0:4], scan_carry[:, 0:4], OP.mult)
                    p.tt("V", lnc_t[:, 12:16], lam_col[:, 4:8], scan_carry[:, 4:8], OP.mult)
                    p.tt("V", lamS[:, 0:4], lnc_t[:, 8:12], lnc_t[:, 12:16], OP.subtract)
                    p.tt("V", lnc_t[:, 8:12], lam_col[:, 0:4], scan_carry[:, 4:8], OP.mult)
                    p.tt("V", lnc_t[:, 12:16], lam_col[:, 4:8], scan_carry[:, 0:4], OP.mult)
                    p.tt("V", lamS[:, 4:8], lnc_t[:, 8:12], lnc_t[:, 12:16], OP.add)
                    for mc in range(8):
                        pb = 6 if mc < 4 else 7
                        p.mm(bank(pb, (mc % 4) * 128, 128), ctm_t[:, mc * 128:(mc + 1) * 128], tri_sb[:])
                    for mc in range(8):
                        pb = 6 if mc < 4 else 7
                        p.act(ctm_t[:, mc * 128:(mc + 1) * 128], bank(pb, (mc % 4) * 128, 128),
                              AF.Identity, bias=lamS[:, mc:mc + 1])
                    stb_t = qb5_t[:].rearrange("p a b -> p (a b)")
                    for cc in range(4):
                        zre = ctm_t[:, cc * 128:(cc + 1) * 128]
                        zim = ctm_t[:, (4 + cc) * 128:(5 + cc) * 128]
                        lre = lp_re[:, cc * 128:(cc + 1) * 128]
                        lim = lp_im[:, cc * 128:(cc + 1) * 128]
                        p.tt("V", qb5_t[:, 3, 0:128], lre, zre, OP.mult)
                        p.tt("V", qb5_t[:, 4, 0:128], lim, zim, OP.mult)
                        p.tt("V", stb_t[:, cc * 128:(cc + 1) * 128], qb5_t[:, 3, 0:128],
                             qb5_t[:, 4, 0:128], OP.subtract)
                        p.tt("V", qb5_t[:, 3, 0:128], lre, zim, OP.mult)
                        p.tt("V", qb5_t[:, 4, 0:128], lim, zre, OP.mult)
                        p.tt("V", stb_t[:, (4 + cc) * 128:(5 + cc) * 128], qb5_t[:, 3, 0:128],
                             qb5_t[:, 4, 0:128], OP.add)
                    p.copy("V", scan_carry[:, 0:8],
                           stb_t[:, 0:1024].rearrange("p (c u) -> p c u", u=128)[:, :, 127])
                    ob = OUTB[tc]
                    for nc8 in range(8):
                        p.mm(bank(ob[0], 0, 512), stb_t[:, nc8 * 128:(nc8 + 1) * 128],
                             wcc_t[:, nc8, 0:512], start=(nc8 == 0), stop=False)
                        p.mm(bank(ob[1], 0, 512), stb_t[:, nc8 * 128:(nc8 + 1) * 128],
                             wcc_t[:, nc8, 512:1024], start=(nc8 == 0), stop=False)
                    for kc in range(8):
                        p.mm(bank(ob[0], 0, 512), xn_fm[:, kc, tsl], wdt_t[:, kc, 0:512],
                             start=False, stop=(kc == 7))
                        p.mm(bank(ob[1], 0, 512), xn_fm[:, kc, tsl], wdt_t[:, kc, 512:1024],
                             start=False, stop=(kc == 7))
                    p.tt("V", x_t[:, tc, 0:512], x_t[:, tc, 0:512], bank(ob[0], 0, 512), OP.add)
                    p.tt("V", x_t[:, tc, 512:1024], x_t[:, tc, 512:1024], bank(ob[1], 0, 512), OP.add)
        p.barrier()

        if 1 not in layers:
            p.load(yout[seq, t0:t0 + TILE, :].rearrange("(c q) f -> q c f", q=128), x_t[:], "dY")
            return

        with ExitStack() as _ph:
            _en = _ph.enter_context
            wqv_t = _en(nc.sbuf_tensor(f"wqv_{seq}_{ti}", [128, 3, 8, 1024], BF16))
            wbb_t = _en(nc.sbuf_tensor(f"wbb_{seq}_{ti}", [128, 8, 4], BF16))
            xnf1_t = _en(nc.sbuf_tensor(f"xnf1_{seq}_{ti}", [128, 1024], F32))
            xnb1_t = _en(nc.sbuf_tensor(f"xnb1_{seq}_{ti}", [128, 1024], BF16))
            lnc1_t = _en(nc.sbuf_tensor(f"lnc1_{seq}_{ti}", [128, 16], F32))
            lndmp_t = _en(nc.sbuf_tensor(f"lndmp_{seq}_{ti}", [128, 1024], F32))
            cvin_t = _en(nc.sbuf_tensor(f"cvin_{seq}_{ti}", [128, 2, 8, 516], BF16))
            cacc_t = _en(nc.sbuf_tensor(f"cacc_{seq}_{ti}", [128, 3, 512], BF16))
            p.load(wqv_t[:, 0], s_wq.rearrange("c q w -> q c w"), "dW")
            p.load(wqv_t[:, 1], s_wk.rearrange("c q w -> q c w"), "dW")
            p.load(wqv_t[:, 2], s_wv.rearrange("c q w -> q c w"), "dW")
            p.load(wbb_t[:], s_wb.rearrange("c q w -> q c w"), "dW")
            for tc in range(TC):
                emit_ln(x_t[:, tc, :], xnf1_t[:], xnb1_t[:], lnc1_t[:], lndmp_t[:])
                emit_trans_to_fm(xnb1_t[:], xn_fm[:], tc, 8)
            tails = (qtail, ktail, vtail)
            for pj in range(3):
                cv = cvin_t[:, 0] if pj == 0 else (cvin_t[:, 1] if pj == 1 else cvin_t[:, 0])
                for oc in range(8):
                    pb = 4 + (oc % 2)
                    for kc in range(8):
                        p.mm(bank(pb, 0, 512), wqv_t[:, pj, kc, oc * 128:(oc + 1) * 128],
                             xn_fm[:, kc, :], start=(kc == 0), stop=(kc == 7))
                    p.copy("V", cv[:, oc, 1:4], tails[pj][:, oc, :])
                    p.copy("A", cv[:, oc, 4:516], bank(pb, 0, 512))
                    p.copy("V", tails[pj][:, oc, :], cv[:, oc, 513:516])
                    a0 = cacc_t[:, pj, :]; a1 = cacc_t[:, (pj + 1) % 3, :]
                    p.ts("V", a0, cv[:, oc, 1:513], convw_sb[:, oc, pj * 4 + 3:pj * 4 + 4], OP.mult)
                    for d in (2, 1):
                        p.ts("V", a1, cv[:, oc, 4 - d:516 - d], convw_sb[:, oc, pj * 4 + d:pj * 4 + d + 1], OP.mult)
                        p.tt("V", a0, a0, a1, OP.add)
                    p.ts("V", a1, cv[:, oc, 4:516], convw_sb[:, oc, pj * 4:pj * 4 + 1], OP.mult)
                    p.tt("V", a0, a0, a1, OP.add)
                    dst = (q_fm[:, oc, :] if pj == 0 else
                           (k_fm[:, oc, :] if pj == 1 else vconv[:, oc, 64:576]))
                    p.act(dst, a0, AF.Silu)
            for kc in range(8):
                p.mm(PS[7][0:4, 0:512], wbb_t[:, kc, :], xn_fm[:, kc, :],
                     start=(kc == 0), stop=(kc == 7))
            p.act(beta_f[:], PS[7][0:4, 0:512], AF.Sigmoid)
            for tc in range(TC):
                p.trans(PS[6][:].bitcast(BF16)[:, 0:4], beta_f[0:4, tc * 128:(tc + 1) * 128], ident_b[0:4, 0:4])
                p.copy("A", beta_tm[:, tc, :], PS[6][:].bitcast(BF16)[:, 0:4])
        p.barrier()

        with ExitStack() as _ph:
            _en = _ph.enter_context
            sqb_t = _en(nc.sbuf_tensor(f"sqb_{seq}_{ti}", [128, 512], BF16))
            rsqb_t = _en(nc.sbuf_tensor(f"rsqb_{seq}_{ti}", [4, 512], BF16))
            rskb_t = _en(nc.sbuf_tensor(f"rskb_{seq}_{ti}", [4, 512], BF16))
            kn_t = _en(nc.sbuf_tensor(f"kn_f_{seq}_{ti}", [128, 8, 512], BF16))
            rsq_f = _en(nc.sbuf_tensor(f"rsq_f_{seq}_{ti}", [4, TILE], F32))
            rsk_f = _en(nc.sbuf_tensor(f"rsk_f_{seq}_{ti}", [4, TILE], F32))
            for (src, rsb, rsf) in ((q_fm, rsqb_t, rsq_f[:]), (k_fm, rskb_t, rsk_f[:])):
                for kc in range(8):
                    p.tt("V", sqb_t[:], src[:, kc, :], src[:, kc, :], OP.mult)
                    p.mm(PS[7][0:4, 0:512], ind_sb[:, kc, :], sqb_t[:],
                         start=(kc == 0), stop=(kc == 7))
                p.ts("V", rsf, PS[7][0:4, 0:512], 1e-12, OP.add)
                p.recip(rsf, rsf)
                p.act(rsb[:], rsf, AF.Sqrt)
            for tc in range(TC):
                p.trans(PS[6][:].bitcast(BF16)[:, 0:4], rskb_t[0:4, tc * 128:(tc + 1) * 128], ident_b[0:4, 0:4])
                p.copy("A", rsk_tm[:, tc, :], PS[6][:].bitcast(BF16)[:, 0:4])
            for kc in range(8):
                h = kc // 2
                p.mm(bank(6, 0, 512), sel4[0:4, h * 128:(h + 1) * 128], rsqb_t[:])
                p.tt("V", fm2[:, kc, :], q_fm[:, kc, :], bank(6, 0, 512), OP.mult)
                p.mm(bank(5, 0, 512), sel4[0:4, h * 128:(h + 1) * 128], rskb_t[:])
                p.tt("V", kn_t[:, kc, :], k_fm[:, kc, :], bank(5, 0, 512), OP.mult)
                p.mm(bank(4, 0, 512), sel4[0:4, h * 128:(h + 1) * 128], beta_f[:])
                p.tt("V", kb_fm[:, kc, :], kn_t[:, kc, :], bank(4, 0, 512), OP.mult)
            for kc in range(8):
                p.copy("A", k_fm[:, kc, :], kn_t[:, kc, :])
        p.barrier()

        with ExitStack() as _ph:
            _en = _ph.enter_context
            w1as_t = _en(nc.sbuf_tensor(f"w1as_{seq}_{ti}", [128, 8, 1024], BF16))
            wos_t = _en(nc.sbuf_tensor(f"wos_{seq}_{ti}", [128, 8, 1024], BF16))
            wef_t = _en(nc.sbuf_tensor(f"wef_{seq}_{ti}", [128, 8, 1024], BF16))
            wpjs_t = _en(nc.sbuf_tensor(f"wpjs_{seq}_{ti}", [128, 8, 1024], BF16))
            vtm_t = _en(nc.sbuf_tensor(f"vtm_{seq}_{ti}", [128, 1024], BF16))
            kntm_t = _en(nc.sbuf_tensor(f"kntm_{seq}_{ti}", [128, 1024], BF16))
            vbtm_t = _en(nc.sbuf_tensor(f"vbtm_{seq}_{ti}", [128, 1024], BF16))
            kbtm_t = _en(nc.sbuf_tensor(f"kbtm_{seq}_{ti}", [128, 1024], BF16))
            dB_t = _en(nc.sbuf_tensor(f"dB_{seq}_{ti}", [128, 512], BF16))
            dBt_t = _en(nc.sbuf_tensor(f"dBt_{seq}_{ti}", [128, 512], BF16))
            dBq_t = _en(nc.sbuf_tensor(f"dBq_{seq}_{ti}", [128, 1, 512], BF16))
            dBqt_t = _en(nc.sbuf_tensor(f"dBqt_{seq}_{ti}", [128, 1, 512], BF16))
            dSt_t = _en(nc.sbuf_tensor(f"dSt_{seq}_{ti}", [128, 512], BF16))
            dAt_t = _en(nc.sbuf_tensor(f"dAt_{seq}_{ti}", [128, 512], BF16))
            dup_t = _en(nc.sbuf_tensor(f"dup_{seq}_{ti}", [128, 1024], BF16))
            dwn_t = _en(nc.sbuf_tensor(f"dwn_{seq}_{ti}", [128, 1024], BF16))
            sbf_t = _en(nc.sbuf_tensor(f"sbf_{seq}_{ti}", [128, 8, 256], BF16))
            stt_t = _en(nc.sbuf_tensor(f"stt__{seq}_{ti}", [128, 4, 32], BF16))
            stf4_t = _en(nc.sbuf_tensor(f"stf4_{seq}_{ti}", [32, 4, 128], BF16))
            stcol_t = _en(nc.sbuf_tensor(f"stcol_{seq}_{ti}", [128, 16], F32))
            stdmp_t = _en(nc.sbuf_tensor(f"stdmp_{seq}_{ti}", [128, 256], F32))
            prb_t = _en(nc.sbuf_tensor(f"prb_{seq}_{ti}", [128, 8], F32))
            hsa_t = _en(nc.sbuf_tensor(f"hsa_{seq}_{ti}", [128, 8, 128], BF16))
            hgb_t = _en(nc.sbuf_tensor(f"hgb_{seq}_{ti}", [128, 128], BF16))
            lgf_t = _en(nc.sbuf_tensor(f"lgf_{seq}_{ti}", [4, 512], BF16))
            smx_t = _en(nc.sbuf_tensor(f"smx_{seq}_{ti}", [128, 6], F32))
            oc2_t = _en(nc.sbuf_tensor(f"oc2_{seq}_{ti}", [128, 2, 256], BF16))
            ofm_t = _en(nc.sbuf_tensor(f"ofm_{seq}_{ti}", [128, 8, 128], BF16))
            dor2_t = _en(nc.sbuf_tensor(f"dor2_{seq}_{ti}", [128, 2048], BF16))
            fac_r = _en(nc.sbuf_tensor(f"facr_{seq}_{ti}", [128, 2, 512], BF16))
            r2g_t = dor2_t.ap().rearrange("p (a b) -> p a b", b=512)
            do_tm_t = dor2_t.ap()[:, 0:1024]
            p.load(w1as_t[:], s_w1a.rearrange("c q w -> q c w"), "dW")
            p.load(wos_t[:], s_wo.rearrange("c q w -> q c w"), "dW")
            p.memset("V", stt_t[:].rearrange("p a b -> p (a b)"), 0.0)
            accs = (dup_t.ap()[:, 0:512], dup_t.ap()[:, 512:1024])
            rsc = dwn_t.ap()[:, 0:512]

            def fir_taps(oc, ndm1, w0):
                for i, d in enumerate(range(ndm1, -1, -1)):
                    a = accs[i % 2]
                    src = vconv[:, oc, 64 - d:576 - d]
                    wcol = firw_sb[:, oc, w0 + d:w0 + d + 1]
                    if i < 2:
                        p.ts("V", a, src, wcol, OP.mult)
                    else:
                        p.stt(a, src, wcol, a, OP.mult, OP.add)

            def r2blk(b):
                for j4 in range(4):
                    ocf = (b % 2) * 4 + j4
                    for kc in range(8):
                        p.mm(PS[4 + j4][:, 0:512], wef_t[:, kc, ocf * 128:(ocf + 1) * 128],
                             xn_fm[:, kc, 0:512], start=(kc == 0), stop=(kc == 7))

            def evacblk(b):
                for j4 in range(4):
                    p.ts("V", rsc, PS[4 + j4][:], 0.0, OP.max)
                    p.tt("V", r2g_t[:, j4, :], rsc, PS[4 + j4][:], OP.mult)

            def downblk(b):
                for tc_ in range(TC):
                    fb_ = OUTB[tc_]
                    for j4 in range(4):
                        jj = (b % 2) * 4 + j4
                        p.mm(PS[fb_[0]][:, 0:512], r2g_t[:, j4, tc_ * 128:(tc_ + 1) * 128],
                             wpjs_t[:, jj, 0:512], start=(j4 == 0), stop=(j4 == 3))
                        p.mm(PS[fb_[1]][:, 0:512], r2g_t[:, j4, tc_ * 128:(tc_ + 1) * 128],
                             wpjs_t[:, jj, 512:1024], start=(j4 == 0), stop=(j4 == 3))
                    p.tt("V", x_t[:, tc_, 0:512], x_t[:, tc_, 0:512], PS[fb_[0]][:, 0:512], OP.add)
                    p.tt("V", x_t[:, tc_, 512:1024], x_t[:, tc_, 512:1024], PS[fb_[1]][:, 0:512], OP.add)

            def transblk(oc):
                for sl, dst_tm in ((0, ll_tm), (1, ls_tm)):
                    for tc_ in range(TC):
                        p.trans(PS[2][:].bitcast(BF16)[:, 0:128],
                                fac_r[:, sl, tc_ * 128:(tc_ + 1) * 128], ident_b)
                        p.copy("A", dst_tm[:, tc_, oc * 128:(oc + 1) * 128],
                               PS[2][:].bitcast(BF16)[:, 0:128])

            for k in range(9):
                pqb = k - 1
                if pqb >= 0:
                    evacblk(pqb)
                if k < 8:
                    if k % 2 == 0:
                        mg = k // 2
                        p.load(wef_t[:], s_weffT.rearrange("c q w -> q c w")[:, :, mg * 1024:(mg + 1) * 1024], "dG", qe="G")
                    r2blk(k)
                    fir_taps(k, 63, 5)
                if pqb >= 0:
                    downblk(pqb)
                    transblk(pqb)
                if k < 8:
                    if k % 2 == 0:
                        p.load(wpjs_t[:], s_wpjT[(k // 2) * 8:(k // 2 + 1) * 8].rearrange("c q w -> q c w"), "dG", qe="G")
                    p.tt("V", fac_r[:, 0, :], accs[0], accs[1], OP.add)
                    fir_taps(k, 4, 0)
                    p.tt("V", fac_r[:, 1, :], accs[0], accs[1], OP.add)
            for oc in range(8):
                p.copy("V", vconv[:, oc, 0:64], vconv[:, oc, 512:576])
            bfv = lambda pb: PS[pb][:].bitcast(BF16)
            for tc in range(TC):
                tsl = slice(tc * 128, (tc + 1) * 128)
                fb = OUTB[tc]
                for kc in range(8):
                    p.trans(bfv(6)[:, (kc % 4) * 256:(kc % 4) * 256 + 128],
                            vconv[:, kc, 64 + tc * 128:64 + (tc + 1) * 128], ident_b)
                    if kc % 4 == 3:
                        p.copy("A", vtm_t[:].rearrange("p (c u) -> p c u", u=128)[:, kc - 3:kc + 1, :],
                               bfv(6).rearrange("p (a b) -> p a b", b=256)[:, 0:4, 0:128])
                for kc in range(8):
                    p.trans(bfv(7)[:, (kc % 4) * 256:(kc % 4) * 256 + 128],
                            k_fm[:, kc, tsl], ident_b)
                    if kc % 4 == 3:
                        p.copy("A", kntm_t[:].rearrange("p (c u) -> p c u", u=128)[:, kc - 3:kc + 1, :],
                               bfv(7).rearrange("p (a b) -> p a b", b=256)[:, 0:4, 0:128])
                for h in range(NH):
                    p.ts("V", vbtm_t[:, h * 256:(h + 1) * 256], vtm_t[:, h * 256:(h + 1) * 256],
                         beta_tm[:, tc, h:h + 1], OP.mult)
                    p.ts("V", kbtm_t[:, h * 256:(h + 1) * 256], kntm_t[:, h * 256:(h + 1) * 256],
                         beta_tm[:, tc, h:h + 1], OP.mult)
                for i8 in range(8):
                    p.copy("A", sbf_t[:, i8, :], Sst[:, i8, :])
                for h in range(NH):
                    p.mm(PS[4][:, h * 128:(h + 1) * 128], kb_fm[:, 2 * h, tsl], k_fm[:, 2 * h, tsl],
                         start=True, stop=False)
                    p.mm(PS[4][:, h * 128:(h + 1) * 128], kb_fm[:, 2 * h + 1, tsl], k_fm[:, 2 * h + 1, tsl],
                         start=False, stop=True)
                p.tt("V", dB_t[:].rearrange("p (a b) -> p a b", b=128),
                     PS[4][:].rearrange("p (a b) -> p a b", b=128),
                     masksl[:].unsqueeze(1).broadcast_to((128, 4, 128)), OP.mult)
                for h in range(NH):
                    p.trans(bfv(5)[:, h * 256:h * 256 + 128], dB_t[:, h * 128:(h + 1) * 128], ident_b)
                p.copy("A", dBt_t[:].rearrange("p (a b) -> p a b", b=128),
                       bfv(5).rearrange("p (a b) -> p a b", b=256)[:, 0:4, 0:128])
                for h in range(NH):
                    p.mm(PS[4][:, h * 128:(h + 1) * 128], k_fm[:, 2 * h, tsl], fm2[:, 2 * h, tsl],
                         start=True, stop=False)
                    p.mm(PS[4][:, h * 128:(h + 1) * 128], k_fm[:, 2 * h + 1, tsl], fm2[:, 2 * h + 1, tsl],
                         start=False, stop=True)
                p.tt("V", dAt_t[:].rearrange("p (a b) -> p a b", b=128),
                     PS[4][:].rearrange("p (a b) -> p a b", b=128),
                     maskui[:].unsqueeze(1).broadcast_to((128, 4, 128)), OP.mult)
                p.tt("V", dSt_t[:], i4_sb[:], dBt_t[:], OP.add)
                curp, curt = dB_t[:], dBt_t[:]
                for lvl in range(6):
                    if lvl % 2 == 0:
                        nxtp = dBq_t[:, 0, :]; nxtt = dBqt_t[:, 0, :]
                    else:
                        nxtp = dB_t[:]; nxtt = dBt_t[:]
                    for h in range(NH):
                        p.mm(PS[4][:, h * 128:(h + 1) * 128], curt[:, h * 128:(h + 1) * 128],
                             curp[:, h * 128:(h + 1) * 128], start=True, stop=True)
                    p.copy("A", nxtp, PS[4][:])
                    if lvl < 5:
                        for h in range(NH):
                            p.mm(PS[5][:, h * 128:(h + 1) * 128], curp[:, h * 128:(h + 1) * 128],
                                 curt[:, h * 128:(h + 1) * 128], start=True, stop=True)
                        p.copy("A", nxtt, PS[5][:])
                    for h in range(NH):
                        p.mm(PS[6][:, h * 128:(h + 1) * 128], nxtp[:, h * 128:(h + 1) * 128],
                             dSt_t[:, h * 128:(h + 1) * 128], start=True, stop=False)
                        p.mm(PS[6][:, h * 128:(h + 1) * 128], ident_b[:],
                             dSt_t[:, h * 128:(h + 1) * 128], start=False, stop=True)
                    p.copy("A", dSt_t[:], PS[6][:])
                    curp, curt = nxtp, nxtt
                for h in range(NH):
                    pb = 6 + h // 2
                    for mc in range(2):
                        p.mm(PS[pb][:, ((h % 2) * 2 + mc) * 128:((h % 2) * 2 + mc + 1) * 128],
                             kbtm_t[:, h * 256 + mc * 128:h * 256 + (mc + 1) * 128],
                             dSt_t[:, h * 128:(h + 1) * 128], start=True, stop=True)
                for pb in (6, 7):
                    p.act(dwn_t[:, (pb - 6) * 512:(pb - 5) * 512], PS[pb][:], AF.Identity, scale=-1.0)
                for h in range(NH):
                    pb = 4 + h // 2
                    off = (h % 2) * 256
                    p.mm(PS[pb][:, off:off + 256], dSt_t[:, h * 128:(h + 1) * 128],
                         vbtm_t[:, h * 256:(h + 1) * 256], start=True, stop=False)
                    for mc in range(2):
                        p.mm(PS[pb][:, off:off + 256], dwn_t[:, (h * 2 + mc) * 128:(h * 2 + mc + 1) * 128],
                             sbf_t[:, h * 2 + mc, :], start=False, stop=(mc == 1))
                p.copy("A", dup_t[:, 0:512], PS[4][:])
                p.copy("A", dup_t[:, 512:1024], PS[5][:])
                for h in range(NH):
                    pb = 4 + h // 2
                    off = (h % 2) * 256
                    p.mm(PS[pb][:, off:off + 256], fm2[:, 2 * h, tsl], sbf_t[:, h * 2, :],
                         start=True, stop=False)
                    p.mm(PS[pb][:, off:off + 256], fm2[:, 2 * h + 1, tsl], sbf_t[:, h * 2 + 1, :],
                         start=False, stop=False)
                    p.mm(PS[pb][:, off:off + 256], dAt_t[:, h * 128:(h + 1) * 128],
                         dup_t[:, h * 256:(h + 1) * 256], start=False, stop=True)
                p.copy("A", do_tm_t[:, 0:512], PS[4][:])
                p.copy("A", do_tm_t[:, 512:1024], PS[5][:])
                for hh in range(2):
                    for h in (2 * hh, 2 * hh + 1):
                        pb = 6 + (h % 2)
                        for mc in range(2):
                            p.mm(PS[pb][:, mc * 256:(mc + 1) * 256],
                                 kntm_t[:, h * 256 + mc * 128:h * 256 + (mc + 1) * 128],
                                 dup_t[:, h * 256:(h + 1) * 256], start=True, stop=True)
                    for h in (2 * hh, 2 * hh + 1):
                        pb = 6 + (h % 2)
                        for mc in range(2):
                            p.tt("V", Sst[:, h * 2 + mc, :], Sst[:, h * 2 + mc, :],
                                 PS[pb][:, mc * 256:(mc + 1) * 256], OP.add)
                srcs = (ls_tm[:, tc, :], ll_tm[:, tc, :], do_tm_t[:], vtm_t[:])
                for ti_ in range(4):
                    src = srcs[ti_]
                    for h in range(NH):
                        sl = src.rearrange("p (h d) -> p h d", d=256)[:, h, :]
                        c0 = stcol_t[:, 0:1]; c1 = stcol_t[:, 1:2]; c2 = stcol_t[:, 2:3]
                        p.red(c0, sl, OP.add)
                        ttr(stdmp_t[:], sl, sl, c1)
                        p.red(c2, sl, OP.add, absval=True)
                        p.ts("V", smx_t[:, 0:1], c0, 1.0 / 256, OP.mult)
                        p.ts("V", smx_t[:, 1:2], c1, 1.0 / 256, OP.mult)
                        p.tt("V", smx_t[:, 2:3], smx_t[:, 0:1], smx_t[:, 0:1], OP.mult)
                        p.tt("V", smx_t[:, 2:3], smx_t[:, 1:2], smx_t[:, 2:3], OP.subtract)
                        p.ts("V", smx_t[:, 3:4], c2, 1.0 / 256, OP.mult)
                        p.act(smx_t[:, 4:5], c1, AF.Sqrt)
                        p.copy("V", stt_t[:, h, ti_ * 4:ti_ * 4 + 1], smx_t[:, 0:1])
                        p.copy("V", stt_t[:, h, ti_ * 4 + 1:ti_ * 4 + 2], smx_t[:, 2:3])
                        p.copy("V", stt_t[:, h, ti_ * 4 + 2:ti_ * 4 + 3], smx_t[:, 3:4])
                        p.copy("V", stt_t[:, h, ti_ * 4 + 3:ti_ * 4 + 4], smx_t[:, 4:5])
                for h in range(NH):
                    p.trans(bfv(6)[0:32, h * 256:h * 256 + 128], stt_t[:, h, :], ident_b)
                p.copy("A", stf4_t[:],
                       bfv(6).rearrange("p (h b) -> p h b", b=256)[0:32, :, 0:128])
                for g4 in range(2):
                    pb = 4 + g4
                    for j in range(4):
                        oc = g4 * 4 + j
                        for kc in range(8):
                            p.mm(PS[pb][:, j * 128:(j + 1) * 128], w1as_t[:, kc, oc * 128:(oc + 1) * 128],
                                 xn_fm[:, kc, tsl], start=(kc == 0), stop=(kc == 7))
                    p.copy("A", hsa_t[:, g4 * 4:g4 * 4 + 4, :],
                           PS[pb][:].rearrange("p (a b) -> p a b", b=128))
                for h in range(NH):
                    for g4 in range(2):
                        pb = 4 + g4
                        for j in range(4):
                            oc = g4 * 4 + j
                            p.mm(PS[pb][:, j * 128:(j + 1) * 128], stf4_t[0:16, h, :],
                                 w1b_sb[:, oc * 128:(oc + 1) * 128], start=True, stop=True)
                    for oc in range(8):
                        pb = 4 + oc // 4
                        j = oc % 4
                        p.tt("V", hgb_t[:], PS[pb][:, j * 128:(j + 1) * 128], hsa_t[:, oc, :], OP.add)
                        p.act(hgb_t[:], hgb_t[:], AF.Gelu, bias=b1_sb[:, oc:oc + 1])
                        p.mm(PS[7][0:4, tc * 0 + h * 128:h * 128 + 128], w2_sb[:, oc, :], hgb_t[:],
                             start=(oc == 0), stop=(oc == 7))
                p.act(lgf_t[:], PS[7][0:4, 0:512], AF.Identity, scale=rtemp_sb[:], bias=b2rt_sb[:])
                for h in range(NH):
                    p.trans(PS[6][:].bitcast(BF16)[:, 0:4], lgf_t[0:4, h * 128:(h + 1) * 128], ident_b[0:4, 0:4])
                    p.copy("V", smx_t[:, 0:4], PS[6][:].bitcast(BF16)[:, 0:4])
                    p.red(smx_t[:, 4:5], smx_t[:, 0:4], OP.max)
                    p.ts("V", smx_t[:, 0:4], smx_t[:, 0:4], smx_t[:, 4:5], OP.subtract)
                    p.act(smx_t[:, 0:4], smx_t[:, 0:4], AF.Exp)
                    p.red(smx_t[:, 4:5], smx_t[:, 0:4], OP.add)
                    p.recip(smx_t[:, 4:5], smx_t[:, 4:5])
                    p.ts("V", smx_t[:, 0:4], smx_t[:, 0:4], smx_t[:, 4:5], OP.mult, 1.0 - 2 * 0.08, OP.mult)
                    p.ts("V", prb_t[:, 4 * 0:1] if False else prb_t[:, 0:1],
                         smx_t[:, 0:1], 0.08, OP.add)
                    p.tt("V", prb_t[:, 0:1], prb_t[:, 0:1], crl_sb[:, h:h + 1], OP.add)
                    p.ts("V", prb_t[:, 1:2], smx_t[:, 1:2], 0.08, OP.add)
                    p.copy("V", prb_t[:, 2:3], smx_t[:, 2:3])
                    p.copy("V", prb_t[:, 3:4], smx_t[:, 3:4])
                    hs4 = slice(h * 256, (h + 1) * 256)
                    p.ts("V", oc2_t[:, 0, :], ls_tm[:, tc, hs4], prb_t[:, 0:1], OP.mult)
                    p.ts("V", oc2_t[:, 1, :], ll_tm[:, tc, hs4], prb_t[:, 1:2], OP.mult)
                    p.tt("V", oc2_t[:, 0, :], oc2_t[:, 0, :], oc2_t[:, 1, :], OP.add)
                    p.ts("V", oc2_t[:, 1, :], do_tm_t[:, hs4], prb_t[:, 2:3], OP.mult)
                    p.tt("V", oc2_t[:, 0, :], oc2_t[:, 0, :], oc2_t[:, 1, :], OP.add)
                    p.ts("V", oc2_t[:, 1, :], vtm_t[:, hs4], prb_t[:, 3:4], OP.mult)
                    p.tt("V", oc2_t[:, 0, :], oc2_t[:, 0, :], oc2_t[:, 1, :], OP.add)
                    ttr(stdmp_t[:], oc2_t[:, 0, :], oc2_t[:, 0, :], smx_t[:, 5:6])
                    p.ts("V", smx_t[:, 5:6], smx_t[:, 5:6], 1.0 / 256, OP.mult, 1e-5, OP.add)
                    p.recip(smx_t[:, 5:6], smx_t[:, 5:6])
                    p.act(smx_t[:, 5:6], smx_t[:, 5:6], AF.Sqrt)
                    p.stt(oc2_t[:, 1, :], oc2_t[:, 0, :], smx_t[:, 5:6], rmsw_sb[:, hs4],
                          OP.mult, OP.mult)
                    for mc in range(2):
                        p.trans(bfv(6)[:, mc * 256:mc * 256 + 128],
                                oc2_t[:, 1, mc * 128:(mc + 1) * 128], ident_b)
                    p.copy("A", ofm_t[:, 2 * h:2 * h + 2, :],
                           bfv(6).rearrange("p (a b) -> p a b", b=256)[:, 0:2, 0:128])
                for kc in range(8):
                    p.mm(PS[fb[0]][:, 0:512], ofm_t[:, kc, :], wos_t[:, kc, 0:512],
                         start=(kc == 0), stop=(kc == 7))
                    p.mm(PS[fb[1]][:, 0:512], ofm_t[:, kc, :], wos_t[:, kc, 512:1024],
                         start=(kc == 0), stop=(kc == 7))
                p.tt("V", x_t[:, tc, 0:512], x_t[:, tc, 0:512], PS[fb[0]][:, 0:512], OP.add)
                p.tt("V", x_t[:, tc, 512:1024], x_t[:, tc, 512:1024], PS[fb[1]][:, 0:512], OP.add)
        p.barrier()
        p.load(yout[seq, t0:t0 + TILE, :].rearrange("(c q) f -> q c f", q=128), x_t[:], "dY")

    for seq in range(nseqs):
        for ti in range(ntiles):
            emit_tile(seq, ti, ti == 0)
    p.qwait("S", "dY", p.dmacnt.get("dY", 0))
    p.finalize()

    with nc.Block() as block:
        @block.sync
        def _(eng):
            for f in p.ops["S"]:
                f(eng)

        @block.vector
        def _(eng):
            for f in p.ops["V"]:
                f(eng)

        @block.scalar
        def _(eng):
            for f in p.ops["A"]:
                f(eng)

        @block.tensor
        def _(eng):
            for f in p.ops["T"]:
                f(eng)

        @block.gpsimd
        def _(eng):
            for f in p.ops["G"]:
                f(eng)
    es.close()
    return nc


_NC_CACHE = {}


def kernel(**inputs):
    key = "full"
    if key not in _NC_CACHE:
        _NC_CACHE[key] = build_program()
    nc = _NC_CACHE[key]
    in_maps = [prep_core_inputs(inputs, c) for c in range(NCORES)]
    res = run_bass_kernel_spmd(nc, in_maps, core_ids=list(range(NCORES)))
    out = np.empty((B, L, D), np.float32)
    for c in range(NCORES):
        out[2 * c:2 * c + 2] = res.results[c]["yout"]
    return out


# version 08 (final)
# speedup vs baseline: 1.0048x; 1.0048x over previous
import math
import numpy as np
import ml_dtypes
import concourse.bass as bass
import concourse.mybir as mybir
from concourse.alu_op_type import AluOpType as OP
from concourse.bass_utils import run_bass_kernel_spmd

F32 = mybir.dt.float32
BF16 = mybir.dt.bfloat16
I32 = mybir.dt.int32
AF = mybir.ActivationFunctionType
AX = mybir.AxisListType

BFNP = ml_dtypes.bfloat16

NCORES = 8
B, L, D = 16, 2048, 1024
SEQ_PER_CORE = 2
TILE = 512
NTILES = L // TILE
TC = TILE // 128
NI = 2816
NH = 4
DK = 256
FFI = 4096
N_SLICE = 512
NSLICES = (NI + N_SLICE - 1) // N_SLICE

TWO_PI = 2.0 * math.pi


class Prog:
    CE = ("T", "V", "A", "G")

    def __init__(self):
        self.ops = {e: [] for e in ("T", "V", "A", "G", "S")}
        self.cnt = {e: 0 for e in self.CE}
        self.dmacnt = {}
        self.dirtyW = {e: set() for e in ("T", "V", "A", "G")}
        self.dirtyR = {e: set() for e in ("T", "V", "A", "G")}
        self.lastwait = {e: {} for e in ("T", "V", "A", "G", "S")}
        self.sems = {}
        self.writer = {}
        self.readers = {}
        self.need_inc = {e: set() for e in self.CE}
        self.inc_cell = {e: {} for e in self.CE}
        self.resolved = {e: {} for e in self.CE}
        self.max_waited = {}
        self.qsynced = {}
        self.dma_bound = {}
        self.dma_resolved = {}

    @staticmethod
    def ap(x):
        if x is None or isinstance(x, (int, float)):
            return x
        if hasattr(x, "then_inc"):
            return x
        try:
            import bass_rust
            if isinstance(x, bass_rust.AP):
                return x
        except Exception:
            pass
        if hasattr(x, "ap") and callable(getattr(x, "ap", None)):
            return x.ap()
        return x

    def _names(self, aps):
        out = []
        for a in aps:
            if a is None:
                continue
            try:
                out.append(a.tensor.name)
            except Exception:
                pass
        return out

    def wait(self, e, sem, val):
        if val <= 0:
            return
        lw = self.lastwait[e]
        if lw.get(sem, -1) >= val:
            return
        lw[sem] = val
        if sem in self.CE:
            self.need_inc[sem].add(val)
            self.ops[e].append(lambda eng, s=sem, v=val: eng.wait_ge(self.sems[s], self.resolved[s][v]))
        else:
            self.max_waited[sem] = max(self.max_waited.get(sem, 0), val)
            self.ops[e].append(lambda eng, s=sem, v=val: eng.wait_ge(
                self.sems[s], self.dma_resolved.get(s, {}).get(v, v)))

    def drain(self, e):
        if self.dirtyW[e] or self.dirtyR[e]:
            self.dirtyW[e] = set()
            self.dirtyR[e] = set()
            self.ops[e].append(lambda eng: eng.drain())

    def _dep_waits(self, e, rd, wr):
        for r in rd:
            w = self.writer.get(r)
            if w and w[0] != e:
                self.wait(e, w[0], w[1])
        for w_ in wr:
            lw = self.writer.get(w_)
            if lw and lw[0] != e:
                self.wait(e, lw[0], lw[1])
            for re_, rc in self.readers.get(w_, {}).items():
                if re_ != e:
                    self.wait(e, re_, rc)

    def _dep_update(self, e, rd, wr, val):
        for w_ in wr:
            self.writer[w_] = (e, val)
            self.readers[w_] = {}
        for r in rd:
            self.readers.setdefault(r, {})[e] = val

    def op(self, e, fn, reads=(), writes=(), inc=True, rdaps=(), wraps=()):
        rd = set(self._names(rdaps)) | set(reads)
        wr = set(self._names(wraps)) | set(writes)
        assert e in self.CE
        self._dep_waits(e, rd, wr)
        if (rd & self.dirtyW[e]) or (wr & (self.dirtyW[e] | self.dirtyR[e])):
            self.drain(e)
        self.dirtyW[e] |= wr
        self.dirtyR[e] |= rd
        self.cnt[e] += 1
        c = self.cnt[e]
        self._dep_update(e, rd, wr, c)
        cell = [False]
        self.inc_cell[e][c] = cell
        def run(eng, f=fn, s=e, _cell=cell):
            ins = f(eng)
            if _cell[0]:
                ins.then_inc(self.sems[s], 1)
            return ins
        self.ops[e].append(run)

    def qwait(self, qe, sem, val):
        if val <= 0:
            return
        lw = self.lastwait[qe]
        if lw.get(sem, -1) >= val:
            return
        lw[sem] = val
        if sem in self.CE:
            self.need_inc[sem].add(val)
            self.ops[qe].append(lambda eng, s=sem, v=val: eng.wait_ge(self.sems[s], self.resolved[s][v]))
        else:
            self.max_waited[sem] = max(self.max_waited.get(sem, 0), val)
            self.ops[qe].append(lambda eng, s=sem, v=val: eng.wait_ge(
                self.sems[s], self.dma_resolved.get(s, {}).get(v, v)))

    def load(self, dst_ap, src_ap, sem, qe="S", slow=False):
        dn = self._names([dst_ap]); sn = self._names([src_ap])
        for w_ in dn:
            lw = self.writer.get(w_)
            if lw and lw[0] != sem:
                self.qwait(qe, lw[0], lw[1])
            for re_, rc in self.readers.get(w_, {}).items():
                self.qwait(qe, re_, rc)
        for r in sn:
            w = self.writer.get(r)
            if w:
                self.qwait(qe, w[0], w[1])
        mw = self.max_waited.get(sem, 0)
        if mw > self.qsynced.get(sem, 0):
            tgt = self.dmacnt.get(sem, 0)
            self.qsynced[sem] = tgt
            self.dma_bound.setdefault(sem, set()).add(tgt)
            lw = self.lastwait[qe]
            if lw.get(sem, -1) < tgt:
                lw[sem] = tgt
                self.ops[qe].append(lambda eng, s=sem, v=tgt: eng.wait_ge(self.sems[s], v))
        self.dmacnt.setdefault(sem, 0)
        self.dmacnt[sem] += 16
        v = self.dmacnt[sem]
        self.ops[qe].append(
            lambda eng, d=dst_ap, s=src_ap, sm=sem, sl=slow: eng.dma_start(
                out=d, in_=s, allow_slow_non_contiguous=sl).then_inc(self.sems[sm], 16))
        for w_ in dn:
            self.writer[w_] = (sem, v)
            self.readers[w_] = {}
        for r in sn:
            self.readers.setdefault(r, {})[sem] = v
        return v


    def mark(self, e):
        if self.cnt[e] > 0:
            return self.cnt[e]
        self.cnt[e] += 1
        c = self.cnt[e]
        cell = [False]
        self.inc_cell[e][c] = cell
        def run(eng, s=e, _cell=cell):
            ins = eng.nop()
            if _cell[0]:
                ins.then_inc(self.sems[s], 1)
            return ins
        self.ops[e].append(run)
        return c

    def finalize(self):
        for sem, total in self.dmacnt.items():
            bounds = sorted(self.dma_bound.get(sem, set()) | {total})
            res = {}
            mw = self.max_waited.get(sem, 0)
            import bisect
            for v in range(16, mw + 16, 16):
                i = bisect.bisect_left(bounds, v)
                res[v] = bounds[i] if i < len(bounds) else total
            self.dma_resolved[sem] = res
        for e in self.CE:
            acc = 0
            res = {}
            need = self.need_inc[e]
            for c in range(1, self.cnt[e] + 1):
                if c in need:
                    self.inc_cell[e][c][0] = True
                    acc += 1
                res[c] = acc
            self.resolved[e] = res

    def barrier(self):
        vals = {}
        for e in self.CE:
            vals[e] = self.mark(e)
        for e in self.CE:
            for o in self.CE:
                if o != e:
                    self.wait(e, o, vals[o])
        for o in self.CE:
            self.qwait("S", o, vals[o])
        for sem, tot in list(self.dmacnt.items()):
            if tot > self.qsynced.get(sem, 0):
                self.qsynced[sem] = tot
                self.dma_bound.setdefault(sem, set()).add(tot)
                lw = self.lastwait["S"]
                if lw.get(sem, -1) < tot:
                    lw[sem] = tot
                    self.ops["S"].append(lambda eng, s=sem, v=tot: eng.wait_ge(self.sems[s], v))

    def mm(self, out, lhsT, rhs, start=True, stop=True, inc=True):
        out = self.ap(out); lhsT = self.ap(lhsT); rhs = self.ap(rhs)
        self.op("T", lambda t, o=out, l=lhsT, r=rhs, st=start, sp=stop:
                t.matmul(out=o, lhsT=l, rhs=r, start=st, stop=sp),
                rdaps=(lhsT, rhs), wraps=(out,), inc=inc)

    def trans(self, out, in_, ident, inc=True):
        out = self.ap(out); in_ = self.ap(in_); ident = self.ap(ident)
        self.op("T", lambda t, o=out, i=in_, d=ident: t.transpose(out=o, in_=i, identity=d),
                rdaps=(in_, ident), wraps=(out,), inc=inc)

    def tt(self, e, out, a, b, alu, inc=True):
        out = self.ap(out); a = self.ap(a); b = self.ap(b)
        self.op(e, lambda g, o=out, x=a, y=b, p=alu: g.tensor_tensor(out=o, in0=x, in1=y, op=p),
                rdaps=(a, b), wraps=(out,), inc=inc)

    def ts(self, e, out, a, s1, op0, s2=None, op1=None, inc=True):
        out = self.ap(out); a = self.ap(a); s1 = self.ap(s1); s2 = self.ap(s2)
        def f(g, o=out, x=a, _s1=s1, _s2=s2, _o0=op0, _o1=op1):
            if _o1 is None:
                return g.tensor_scalar(out=o, in0=x, scalar1=_s1, scalar2=None, op0=_o0)
            return g.tensor_scalar(out=o, in0=x, scalar1=_s1, scalar2=_s2, op0=_o0, op1=_o1)
        rd = [a] + [z for z in (s1, s2) if hasattr(z, "tensor")]
        self.op(e, f, rdaps=rd, wraps=(out,), inc=inc)

    def stt(self, out, a, scalar, b, op0, op1, inc=True, e="V"):
        out = self.ap(out); a = self.ap(a); scalar = self.ap(scalar); b = self.ap(b)
        rd = [a, b] + ([scalar] if hasattr(scalar, "tensor") else [])
        self.op(e, lambda g, o=out, x=a, s=scalar, y=b, p0=op0, p1=op1:
                g.scalar_tensor_tensor(out=o, in0=x, scalar=s, in1=y, op0=p0, op1=p1),
                rdaps=rd, wraps=(out,), inc=inc)

    def red(self, out, in_, alu, absval=False, inc=True):
        out = self.ap(out); in_ = self.ap(in_)
        self.op("V", lambda g, o=out, i=in_, p=alu, av=absval:
                g.tensor_reduce(out=o, in_=i, axis=AX.X, op=p, apply_absolute_value=av),
                rdaps=(in_,), wraps=(out,), inc=inc)

    def act(self, out, in_, func, bias=None, scale=None, inc=True, accum=None):
        out = self.ap(out); in_ = self.ap(in_); bias = self.ap(bias); scale = self.ap(scale); accum = self.ap(accum)
        def f(g, o=out, i=in_, fn=func, b=bias, s=scale, ac=accum):
            kw = {}
            if b is not None:
                kw["bias"] = b
            if s is not None:
                kw["scale"] = s
            if ac is not None:
                kw["accum_out"] = ac
            return g.activation(out=o, in_=i, func=fn, **kw)
        rd = [in_] + [z for z in (bias, scale) if hasattr(z, "tensor")]
        self.op("A", f, rdaps=rd, wraps=(out,) + ((accum,) if accum is not None else ()), inc=inc)

    def copy(self, e, out, in_, inc=True):
        out = self.ap(out); in_ = self.ap(in_)
        if e == "A":
            self.act(out, in_, AF.Identity, inc=inc)
        else:
            self.op(e, lambda g, o=out, i=in_: g.tensor_copy(out=o, in_=i),
                    rdaps=(in_,), wraps=(out,), inc=inc)

    def recip(self, out, in_, inc=True):
        out = self.ap(out); in_ = self.ap(in_)
        self.op("V", lambda g, o=out, i=in_: g.reciprocal(out=o, in_=i),
                rdaps=(in_,), wraps=(out,), inc=inc)

    def memset(self, e, ap, val, inc=True):
        ap = self.ap(ap)
        self.op(e, lambda g, a=ap, v=val: g.memset(a, v), wraps=(ap,), inc=inc)



def _bf(x):
    return np.ascontiguousarray(x, dtype=BFNP)


def _f32(x):
    return np.ascontiguousarray(x, dtype=np.float32)


def prep_core_inputs(I, core):
    d = {}
    d["xin"] = _f32(I["L0_attn_x"][2 * core:2 * core + 2])
    d["bcombT"] = _f32(np.concatenate([I["L0_attn_B_re"].T, I["L0_attn_B_im"].T], axis=1))
    d["creT"] = _f32(I["L0_attn_C_re"].T)
    d["cimT"] = _f32(I["L0_attn_C_im"].T)
    d["dT"] = _f32(I["L0_attn_D"].T)
    d["nu_row"] = _f32(I["L0_attn_nu_log"][None, :])
    d["th_row"] = _f32(I["L0_attn_theta_log"][None, :])
    d["gam_row"] = _f32(I["L0_attn_gamma_log"][None, :])
    d["wg_nat"] = _f32(I["L0_ffn_w_gate"])
    d["wu_nat"] = _f32(I["L0_ffn_w_up"])
    d["wd_nat"] = _f32(I["L0_ffn_w_down"])
    d["wq_n"] = _f32(I["L1_attn_Wq"])
    d["wk_n"] = _f32(I["L1_attn_Wk"])
    d["wv_n"] = _f32(I["L1_attn_Wv"])
    d["wb_n"] = _f32(I["L1_attn_Wb"])
    def col_fm(w):
        K = w.shape[1]
        return _f32(w[:, ::-1].reshape(8, 128, K).transpose(1, 0, 2))
    d["convw"] = np.ascontiguousarray(np.concatenate(
        [col_fm(I["L1_attn_conv_w_q"]), col_fm(I["L1_attn_conv_w_k"]),
         col_fm(I["L1_attn_conv_w_v"])], axis=2))
    d["firw"] = np.ascontiguousarray(np.concatenate(
        [col_fm(I["L1_attn_fir_short_w"].reshape(1024, 5)),
         col_fm(I["L1_attn_fir_long_w"].reshape(1024, 64))], axis=2))
    d["w1a"] = _f32(I["L1_attn_mlp_w1"][:1024])
    d["w1b"] = _f32(I["L1_attn_mlp_w1"][1024:])
    d["b1col"] = _f32(I["L1_attn_mlp_b1"].reshape(8, 128).T)
    d["w2_n"] = _f32(I["L1_attn_mlp_w2"])
    d["b2col"] = _f32(I["L1_attn_mlp_b2"][:, None])
    d["ltcol"] = _f32(np.broadcast_to(I["L1_attn_logit_temp"].reshape(1, 1), (4, 1)))
    d["crl_rep"] = _f32(np.broadcast_to(I["L1_attn_conv_res_logit"][None, :], (128, 4)))
    d["rmsw_tile"] = _f32(np.broadcast_to(np.tile(np.asarray(I["L1_attn_rms_w"]), 4)[None, :], (128, 1024)))
    d["wo_n"] = _f32(I["L1_attn_Wo"])
    d["wfc_nat"] = _f32(I["L1_ffn_w_fc"])
    d["wfcT"] = _f32(np.asarray(I["L1_ffn_w_fc"]).T)
    d["wpj_nat"] = _f32(I["L1_ffn_w_proj"])
    d["wpjT"] = _f32(np.asarray(I["L1_ffn_w_proj"]).T)
    d["uT"] = _f32(np.asarray(I["L1_ffn_u"]).T)
    d["fb_n"] = _f32(I["L1_ffn_fast_b"])
    d["v_n"] = _f32(I["L1_ffn_v"])
    d["gate_rep"] = _f32(np.broadcast_to(np.asarray(I["L1_ffn_gate"]).reshape(1, 1), (128, 1)))
    d["identF"] = _f32(np.eye(128))
    d["identB"] = _bf(np.eye(128))
    d["i4"] = _bf(np.tile(np.eye(128), (1, 4)))
    jj, uu = np.mgrid[0:128, 0:128]
    d["tri"] = _bf((jj <= uu).astype(np.float32))
    d["masksl_neg"] = _bf(-(uu < jj).astype(np.float32))
    d["maskui"] = _bf((uu >= jj).astype(np.float32))
    d["ramp"] = _f32(np.arange(128)[None, :])
    d["ones_row"] = _bf(np.ones((1, 128)))
    d["ones_f"] = _f32(np.ones((1, 128)))
    d["ones_col_f"] = _f32(np.ones((128, 1)))
    d["ramp_col"] = _f32(np.arange(128, dtype=np.float32)[:, None])
    sel = np.zeros((4, 512), np.float32)
    for h in range(4):
        sel[h, h * 128:(h + 1) * 128] = 1.0
    d["sel4"] = _bf(sel)
    ind = np.zeros((128, 8, 4), np.float32)
    for c in range(8):
        ind[:, c, c // 2] = 1.0
    d["ind_h"] = _bf(ind)
    return d


def build_program(layers=(0, 1), ntiles=None, nseqs=None):
    if ntiles is None:
        ntiles = NTILES
    if nseqs is None:
        nseqs = SEQ_PER_CORE
    nc = bass.Bass("TRN2", target_bir_lowering=False, debug=False)
    p = Prog()
    from contextlib import ExitStack
    es = ExitStack()

    def din(name, shape, dt=F32):
        return nc.dram_tensor(name, shape, dt, kind="ExternalInput").ap()
    xin = din("xin", [SEQ_PER_CORE, L, D])
    yout = nc.dram_tensor("yout", [SEQ_PER_CORE, L, D], F32, kind="ExternalOutput").ap()
    bcombT = din("bcombT", [D, 1024]); creT = din("creT", [512, D]); cimT = din("cimT", [512, D])
    dT = din("dT", [D, D])
    nu_row = din("nu_row", [1, 512]); th_row = din("th_row", [1, 512]); gam_row = din("gam_row", [1, 512])
    wg_nat = din("wg_nat", [NI, D]); wu_nat = din("wu_nat", [NI, D]); wd_nat = din("wd_nat", [D, NI])
    wq_n = din("wq_n", [D, D]); wk_n = din("wk_n", [D, D]); wv_n = din("wv_n", [D, D]); wb_n = din("wb_n", [D, 4])
    convw = din("convw", [128, 8, 12]); firw = din("firw", [128, 8, 69])
    w1a = din("w1a", [D, D]); w1b_in = din("w1b", [16, D]); b1col = din("b1col", [128, 8])
    w2_n = din("w2_n", [D, 4]); b2col = din("b2col", [4, 1]); ltcol = din("ltcol", [4, 1])
    crl_rep = din("crl_rep", [128, 4]); rmsw_in = din("rmsw_tile", [128, D])
    wo_n = din("wo_n", [D, D])
    wfc_nat = din("wfc_nat", [FFI, D]); wfcT = din("wfcT", [D, FFI])
    wpj_nat = din("wpj_nat", [D, FFI]); wpjT = din("wpjT", [FFI, D])
    uT = din("uT", [16, FFI]); fb_n = din("fb_n", [16, 16]); v_n = din("v_n", [16, D])
    gate_rep = din("gate_rep", [128, 1])
    identF = din("identF", [128, 128]); identB = din("identB", [128, 128], BF16)
    i4 = din("i4", [128, 512], BF16); tri_in = din("tri", [128, 128], BF16)
    masksl_in = din("masksl_neg", [128, 128], BF16); maskui_in = din("maskui", [128, 128], BF16)
    ramp_in = din("ramp", [1, 128]); ones_row_in = din("ones_row", [1, 128], BF16)
    ones_f_in = din("ones_f", [1, 128])
    ones_col_in = din("ones_col_f", [128, 1])
    rampc_in = din("ramp_col", [128, 1])
    sel4_in = din("sel4", [4, 512], BF16)
    ind_in = din("ind_h", [128, 8, 4], BF16)

    def scr(name, shape):
        return nc.dram_tensor(name, shape, BF16, kind="Internal").ap()
    s_bcomb = scr("s_bcomb", [8, 128, 1024])
    s_ccomb = scr("s_ccomb", [8, 128, 1024])
    s_dT = scr("s_dT", [8, 128, 1024])
    s_wgT = scr("s_wgT", [8, 128, NI]); s_wuT = scr("s_wuT", [8, 128, NI])
    s_wdT = scr("s_wdT", [22, 128, 1024])
    s_wq = scr("s_wq", [8, 128, 1024]); s_wk = scr("s_wk", [8, 128, 1024]); s_wv = scr("s_wv", [8, 128, 1024])
    s_wb = scr("s_wb", [8, 128, 4])
    s_w1a = scr("s_w1a", [8, 128, 1024])
    s_wo = scr("s_wo", [8, 128, 1024])
    s_weffT = scr("s_weffT", [8, 128, FFI])
    s_wpjT = scr("s_wpjT", [32, 128, 1024])
    s_row = nc.dram_tensor("s_row", [1, 512], F32, kind="Internal").ap()

    for nm in ("T", "V", "A", "G", "dW", "dX", "dY", "dP", "dG"):
        p.sems[nm] = es.enter_context(nc.semaphore(nm))

    def sb(name, shape, dt=F32):
        return es.enter_context(nc.sbuf_tensor(name, shape, dt))

    ident_f = sb("ident_f", [128, 128]); ident_b = sb("ident_b", [128, 128], BF16)
    i4_sb = sb("i4_sb", [128, 512], BF16); tri_sb = sb("tri_sb", [128, 128], BF16)
    masksl = sb("masksl", [128, 128], BF16); maskui = sb("maskui_s", [128, 128], BF16)
    ramp_sb = sb("ramp_sb", [1, 128]); ones_sb = sb("ones_sb", [1, 128], BF16)
    ones_f = sb("ones_f_s", [1, 128])
    ones_col_f = sb("ones_col_f_s", [128, 1])
    ramp_col = sb("ramp_col_s", [128, 1])
    sel4 = sb("sel4_s", [4, 512], BF16)
    ind_sb = sb("ind_sb", [128, 8, 4], BF16)
    convw_sb = sb("convw_sb", [128, 8, 12]); firw_sb = sb("firw_sb", [128, 8, 69])
    rmsw_sb = sb("rmsw_sb", [128, D], BF16)
    b1_sb = sb("b1_sb", [128, 8]); w1b_sb = sb("w1b_sb", [16, D], BF16)
    w2_sb = sb("w2_sb", [128, 8, 4], BF16)
    b2_sb = sb("b2_sb", [4, 1]); rtemp_sb = sb("rtemp_sb", [4, 1]); b2rt_sb = sb("b2rt_sb", [4, 1])
    crl_sb = sb("crl_sb", [128, 4]); gate_sb = sb("gate_sb", [128, 1])
    lm_re = sb("lm_re", [128, 512], BF16); lm_im = sb("lm_im", [128, 512], BF16)
    lp_re = sb("lp_re", [128, 512], BF16); lp_im = sb("lp_im", [128, 512], BF16)
    lam_col = sb("lam_col", [128, 8])
    x_t = sb("x_t", [128, TC, 1024])
    xn_fm = sb("xn_fm", [128, 8, TILE], BF16)
    fm2 = sb("fm2", [128, 8, TILE], BF16)
    q_fm = sb("q_fm", [128, 8, TILE], BF16)
    k_fm = sb("k_fm", [128, 8, TILE], BF16)
    kb_fm = sb("kb_fm", [128, 8, TILE], BF16)
    vconv = sb("vconv", [128, 8, 64 + TILE], BF16)
    scan_carry = sb("scan_carry", [128, 8])
    lamS = sb("lamS", [128, 8])
    Sst = sb("Sst", [128, 8, DK])
    qtail = sb("qtail", [128, 8, 3], BF16); ktail = sb("ktail", [128, 8, 3], BF16)
    vtail = sb("vtail", [128, 8, 3], BF16)
    ls_tm = sb("ls_tm", [128, TC, 1024], BF16)
    ll_tm = sb("ll_tm", [128, TC, 1024], BF16)
    beta_f = sb("beta_f", [4, TILE], BF16)
    beta_tm = sb("beta_tm", [128, TC, 4])
    rsk_tm = sb("rsk_tm", [128, TC, 4])

    PS = [es.enter_context(nc.psum_tensor(f"ps{i}", [128, 512], F32)) for i in range(8)]

    def bank(i, lo=0, n=512):
        return PS[i][:, lo:lo + n]

    def bankb(i, lo=0, n=1024):
        return PS[i][:].bitcast(BF16)[:, lo:lo + n]

    W = {"v": 0}

    def ld_wait(e, v, sem="dW"):
        p.wait(e, sem, v)

    def quant_fp4(dst, src, N, tm):
        nb = N // 16
        qsc = tm["qsc"][:, :nb]; qrs = tm["qrs"][:, :nb]; qscb = tm["qscb"][:, :nb]
        xs = tm["qf1"][:, :N]; t = tm["qf2"][:, :N]
        A = tm["qb1"][:, :N]; Bb = tm["qb2"][:, :N]; C = tm["qb3"][:, :N]
        Dd = tm["qb4"][:, :N]; E = tm["qb5"][:, :N]
        r3 = lambda ap: ap.rearrange("p (a b) -> p a b", b=16)
        p.red(qsc, r3(src), OP.max, absval=True)
        p.ts("V", qsc, qsc, 1.0 / 6.0, OP.mult, 1e-12, OP.max)
        p.recip(qrs, qsc)
        p.copy("A", qscb, qsc)
        p.tt("V", r3(xs), r3(src), qrs.unsqueeze(2).broadcast_to((128, nb, 16)), OP.mult)
        p.stt(t, xs, -1.0, xs, OP.mult, OP.max)
        p.ts("V", A, t, 0.25, OP.is_gt)
        p.stt(A, t, 0.75, A, OP.is_gt, OP.add)
        p.stt(A, t, 1.25, A, OP.is_gt, OP.add)
        p.stt(A, t, 1.75, A, OP.is_gt, OP.add)
        p.ts("V", Bb, t, 2.5, OP.is_gt)
        p.stt(Bb, t, 3.5, Bb, OP.is_gt, OP.add)
        p.stt(Bb, t, 5.0, Bb, OP.is_gt, OP.add)
        p.stt(Bb, t, 5.0, Bb, OP.is_gt, OP.add)
        p.stt(E, A, 0.5, Bb, OP.mult, OP.add)
        p.ts("V", A, xs, 0.0, OP.is_ge, 2.0, OP.mult)
        p.tt("V", r3(Bb), r3(E), qscb.unsqueeze(2).broadcast_to((128, nb, 16)), OP.mult)
        p.stt(dst, A, -1.0, Bb, OP.add, OP.mult)

    def range_sin(dst, m_ap, tm, phase_quarter=False):
        ti = tm["ti"]; f = tm["tf1"]; g = tm["tf2"]
        src = m_ap
        if phase_quarter:
            p.ts("V", f, src, 0.25, OP.add)
            src = f
        p.ts("V", ti, src, 1.0, OP.mult)
        p.tt("V", f, src, ti, OP.subtract)
        p.ts("V", g, f, 0.5, OP.is_ge)
        p.tt("V", f, f, g, OP.subtract)
        p.act(dst, f, AF.Sin, scale=TWO_PI)

    cl = []
    for dst, src in ((ident_f, identF), (ident_b, identB), (i4_sb, i4), (tri_sb, tri_in),
                     (masksl, masksl_in), (maskui, maskui_in), (ramp_sb, ramp_in),
                     (ones_sb, ones_row_in), (ones_f, ones_f_in), (ones_col_f, ones_col_in), (ramp_col, rampc_in), (sel4, sel4_in), (ind_sb[:].rearrange("p a b -> p (a b)"), ind_in.rearrange("p a b -> p (a b)")),
                     (convw_sb[:].rearrange("p a b -> p (a b)"), convw.rearrange("p a b -> p (a b)")),
                     (firw_sb[:].rearrange("p a b -> p (a b)"), firw.rearrange("p a b -> p (a b)")),
                     (b1_sb, b1col), (b2_sb, b2col),
                     (crl_sb[:], crl_rep), (gate_sb, gate_rep)):
        cl.append(p.load(dst if hasattr(dst, "tensor") else dst[:], src, "dP"))
    p.load(rmsw_sb[:], rmsw_in, "dG", qe="G")
    v_const = p.dmacnt["dP"]
    for e in ("T", "V", "A"):
        p.wait(e, "dP", v_const)

    with ExitStack() as _ph:
        _en = _ph.enter_context
        prow_t = _en(nc.sbuf_tensor("prow", [1, 512], F32))
        prow2_t = _en(nc.sbuf_tensor("prow2", [1, 512], F32))
        prow3_t = _en(nc.sbuf_tensor("prow3", [1, 512], F32))
        prow4_t = _en(nc.sbuf_tensor("prow4", [1, 512], F32))
        pti_t = _en(nc.sbuf_tensor("pti", [128, 512], I32))
        ptf1_t = _en(nc.sbuf_tensor("ptf1", [128, 512], F32))
        ptf2_t = _en(nc.sbuf_tensor("ptf2", [128, 512], F32))
        pexp_t = _en(nc.sbuf_tensor("pexp", [128, 512], F32))
        psin_t = _en(nc.sbuf_tensor("psin", [128, 512], F32))
        pcos_t = _en(nc.sbuf_tensor("pcos", [128, 512], F32))
        pgc_t = _en(nc.sbuf_tensor("pgc", [128, 8], F32))
        prow = prow_t[:]; prow2 = prow2_t[:]; prow3 = prow3_t[:]; prow4 = prow4_t[:]
        tmr = {"ti": pti_t[:], "tf1": ptf1_t[:], "tf2": ptf2_t[:]}
        eb = pexp_t; pb2 = psin_t; gb = pcos_t
        v1 = p.load(eb[:], nu_row.partition_broadcast(128).squeeze(1), "dP")
        v2 = p.load(pb2[:], th_row.partition_broadcast(128).squeeze(1), "dP")
        v3 = p.load(gb[:], gam_row.partition_broadcast(128).squeeze(1), "dP")
        for e in ("T", "V", "A"):
            p.wait(e, "dP", max(v1, v2, v3))
        p.act(eb[:], eb[:], AF.Exp)
        p.act(pb2[:], pb2[:], AF.Exp)
        p.act(gb[:], gb[:], AF.Exp)
        tmq = {"ti": pti_t[:, 0:128], "tf1": ptf1_t[:, 0:128], "tf2": ptf2_t[:, 0:128]}
        lptmp_re = prow_t; lptmp_im = prow2_t
        for cc in range(4):
            sl = slice(cc * 128, (cc + 1) * 128)
            m1 = ptf1_t[:, 128:256]; m2 = ptf1_t[:, 256:384]
            sinb = ptf2_t[:, 128:256]; cosb = ptf2_t[:, 256:384]; exb = ptf2_t[:, 384:512]
            p.ts("V", m1, eb[:, sl], ramp_col[:], OP.mult)
            p.ts("V", m2, pb2[:, sl], ramp_col[:], OP.mult, 1.0 / TWO_PI, OP.mult)
            p.act(exb, m1, AF.Exp)
            range_sin(sinb, m2, tmq)
            range_sin(cosb, m2, tmq, phase_quarter=True)
            p.tt("V", m1, exb, cosb, OP.mult)
            p.tt("V", lm_re[:, sl], m1, gb[:, sl], OP.mult)
            p.stt(m1, sinb, -1.0, exb, OP.mult, OP.mult)
            p.tt("V", lm_im[:, sl], m1, gb[:, sl], OP.mult)
            p.ts("V", m1, eb[:, sl], ramp_col[:], OP.mult)
            p.act(exb, m1, AF.Exp, scale=-1.0)
            p.tt("V", m1, exb, cosb, OP.mult)
            p.tt("V", m2, exb, sinb, OP.mult)
            p.trans(PS[6][:, 0:128], m1, ident_f)
            p.trans(PS[6][:, 128:256], m2, ident_f)
            p.copy("A", lp_re[:, sl], PS[6][:, 0:128])
            p.copy("A", lp_im[:, sl], PS[6][:, 128:256])
            p.copy("V", lam_col[:, cc:cc + 1], lp_re[:, cc * 128 + 1:cc * 128 + 2])
            p.copy("V", lam_col[:, 4 + cc:5 + cc], lp_im[:, cc * 128 + 1:cc * 128 + 2])

    def emit_weight_prologue():
      with ExitStack() as _ph:
        _en = _ph.enter_context
        wpin_t = _en(nc.sbuf_tensor("wpin", [128, 1, FFI], F32))
        wpout_t = _en(nc.sbuf_tensor("wpout", [128, 1, FFI], BF16))
        prqsc_t = _en(nc.sbuf_tensor("prqsc", [128, 1, 512], F32))
        prqf_t = _en(nc.sbuf_tensor("prqf", [128, 2, 704], F32))
        prqb_t = _en(nc.sbuf_tensor("prqb", [128, 5, 704], BF16))
        prrow_t = _en(nc.sbuf_tensor("prrow", [1, 512], F32))
        prbc_t = _en(nc.sbuf_tensor("prbc", [128, FFI], BF16))
        prsc_t = _en(nc.sbuf_tensor("prsc", [128, 32], F32))
        prvf_t = _en(nc.sbuf_tensor("prvf", [128, FFI], BF16))
        rbcq_t = _en(nc.sbuf_tensor("rbcq", [128, FFI], F32))

        tmpq = {"qsc": prqsc_t[:, 0, 0:176], "qrs": prqsc_t[:, 0, 176:352],
                "qscb": prqsc_t[:, 0, 352:440].bitcast(BF16),
                "qf1": prqf_t[:, 0, :], "qf2": prqf_t[:, 1, :],
                "qb1": prqb_t[:, 0, :], "qb2": prqb_t[:, 1, :], "qb3": prqb_t[:, 2, :],
                "qb4": prqb_t[:, 3, :], "qb5": prqb_t[:, 4, :]}

        def cast_stream(dst_scr, src_dram, nch, width, scale_cols=None, dst_off=0):
            for c in range(nch):
                src = src_dram.rearrange("(n q) w -> n q w", q=128)[c]
                p.load(wpin_t[:, 0, 0:width], src, "dP")
                if scale_cols is None:
                    p.act(wpout_t[:, 0, 0:width], wpin_t[:, 0, 0:width], AF.Identity)
                else:
                    p.act(wpout_t[:, 0, 0:width], wpin_t[:, 0, 0:width], AF.Identity,
                          scale=scale_cols[c])
                p.load(dst_scr[dst_off + c], wpout_t[:, 0, 0:width], "dP")

        cast_stream(s_bcomb, bcombT, 8, 1024)
        cast_stream(s_dT, dT, 8, 1024)
        cast_stream(s_ccomb, creT, 4, 1024)
        cast_stream(s_ccomb, cimT, 4, 1024, scale_cols=[-1.0] * 4, dst_off=4)
        cast_stream(s_wq, wq_n, 8, 1024)
        cast_stream(s_wk, wk_n, 8, 1024)
        cast_stream(s_wv, wv_n, 8, 1024)
        cast_stream(s_wb, wb_n, 8, 4)
        cast_stream(s_w1a, w1a, 8, 1024)
        cast_stream(s_wo, wo_n, 8, 1024)
        p.load(wpin_t[0:16, 0, 0:1024], w1b_in, "dP")
        p.copy("V", w1b_sb[:], wpin_t[0:16, 0, 0:1024])
        p.load(wpin_t[:, 0, 1024:1056].rearrange("p (c w) -> p c w", w=4), w2_n.rearrange("(c q) w -> q c w", q=128), "dP")
        p.copy("V", w2_sb[:].rearrange("p a b -> p (a b)"), wpin_t[:, 0, 1024:1056])
        p.load(rtemp_sb[:], ltcol, "dP")
        p.ts("V", b2rt_sb[:], rtemp_sb[:], -1.0, OP.mult)
        p.tt("V", b2rt_sb[:], b2rt_sb[:], rtemp_sb[:], OP.max)
        p.act(b2rt_sb[:], b2rt_sb[:], AF.Exp, scale=-1.0)
        p.act(b2rt_sb[:], b2rt_sb[:], AF.Ln, bias=1.0)
        p.ts("V", rtemp_sb[:], rtemp_sb[:], 0.0, OP.max)
        p.tt("V", rtemp_sb[:], rtemp_sb[:], b2rt_sb[:], OP.add)
        p.ts("V", rtemp_sb[:], rtemp_sb[:], 1e-6, OP.add)
        p.recip(rtemp_sb[:], rtemp_sb[:])
        p.tt("V", b2rt_sb[:], b2_sb[:], rtemp_sb[:], OP.mult)
        p.act(crl_sb[:], crl_sb[:], AF.Sigmoid)

        def fp4_stream(dst_scr, src_dram, nch, width, kchunks):
            for c in range(nch):
                src = src_dram.rearrange("(n q) w -> n q w", q=128)[c]
                p.load(wpin_t[:, 0, 0:width], src, "dP")
                wq = 704 if width % 704 == 0 else 512
                for q_ in range(width // wq):
                    quant_fp4(wpout_t[:, 0, q_ * wq:(q_ + 1) * wq],
                              wpin_t[:, 0, q_ * wq:(q_ + 1) * wq], wq, tmpq)
                for kc in range(kchunks):
                    pb = 6 + (kc % 2)
                    p.trans(bankb(pb, 0, 128), wpout_t[:, 0, kc * 128:(kc + 1) * 128], ident_b)
                    p.copy("A", prvf_t[:, kc * 128:(kc + 1) * 128], bankb(pb, 0, 128))
                for kc in range(kchunks):
                    p.load(dst_scr[kc][:, c * 128:(c + 1) * 128],
                           prvf_t[:, kc * 128:(kc + 1) * 128], "dP")

        fp4_stream(s_wgT, wg_nat, 22, 1024, 8)
        fp4_stream(s_wuT, wu_nat, 22, 1024, 8)
        fp4_stream(s_wdT, wd_nat, 8, NI, 22)



        def tern_stream(dst_scr, srcT, kchunks, N, kin, extra=None):
            for sl in range(N // 512):
                for kc in range(kin // 128):
                    p.load(wpin_t[:, 0, 0:512],
                           srcT.rearrange("(n q) w -> n q w", q=128)[kc][:, sl * 512:(sl + 1) * 512], "dP")
                    p.ts("V", prqf_t[:, 0, 0:512], wpin_t[:, 0, 0:512], -1.0, OP.mult)
                    p.tt("V", prqf_t[:, 0, 0:512], prqf_t[:, 0, 0:512], wpin_t[:, 0, 0:512], OP.max)
                    p.mm(PS[7][0:1, 0:512], ones_col_f[:], prqf_t[:, 0, 0:512],
                         start=(kc == 0), stop=(kc == kin // 128 - 1))
                p.ts("V", prrow_t[0:1, 0:512], PS[7][0:1, 0:512], 1.0 / kin, OP.mult, 1e-5, OP.max)
                p.load(s_row, prrow_t[0:1, 0:512], "dP")
                p.load(prqf_t[:, 0, 0:512], s_row.partition_broadcast(128).squeeze(1), "dP")
                p.copy("V", prbc_t[:, sl * 512:(sl + 1) * 512], prqf_t[:, 0, 0:512])
                p.recip(prrow_t[0:1, 0:512], prrow_t[0:1, 0:512])
                p.load(s_row, prrow_t[0:1, 0:512], "dP")
                p.load(rbcq_t[:, sl * 512:(sl + 1) * 512], s_row.partition_broadcast(128).squeeze(1), "dP")
            for kc in range(kchunks):
                p.load(wpin_t[:, 0, 0:N], srcT.rearrange("(n q) w -> n q w", q=128)[kc], "dP")
                if extra is not None:
                    extra(kc)
                for q_ in range(N // 512):
                    qs = slice(q_ * 512, (q_ + 1) * 512)
                    t = prqf_t[:, 1, 0:512]
                    p.tt("V", t, wpin_t[:, 0, qs], rbcq_t[:, qs], OP.mult)
                    a = prqb_t[:, 0, 0:512]; sg = prqb_t[:, 1, 0:512]; q1 = prqb_t[:, 2, 0:512]
                    af = prqf_t[:, 0, 0:512]
                    p.ts("V", af, t, -1.0, OP.mult)
                    p.tt("V", af, af, t, OP.max)
                    p.ts("V", a, af, 0.5, OP.is_gt)
                    p.ts("V", sg, t, 0.0, OP.is_ge, 2.0, OP.mult)
                    p.stt(q1, sg, -1.0, a, OP.add, OP.mult)
                    p.tt("V", wpout_t[:, 0, qs], q1, prbc_t[:, qs], OP.mult)
                if extra is not None:
                    p.stt(wpout_t[:, 0, 0:N], prvf_t[:, 0:N], gate_sb[:, 0:1],
                          wpout_t[:, 0, 0:N], OP.mult, OP.add)
                p.load(dst_scr[kc], wpout_t[:, 0, 0:N], "dP")

        with ExitStack() as _ph:
            _en = _ph.enter_context
            lru_t = _en(nc.sbuf_tensor("lru", [16, FFI], BF16))
            lrfb_t = _en(nc.sbuf_tensor("lrfb", [16, 16], BF16))
            lrv_t = _en(nc.sbuf_tensor("lrv", [16, 1024], BF16))
            lrfbu_t = _en(nc.sbuf_tensor("lrfbu", [16, FFI], BF16))
            for q_ in range(2):
                p.load(wpin_t[0:16, 0, 0:2048], uT[:, q_ * 2048:(q_ + 1) * 2048], "dP")
                p.copy("V", lru_t[0:16, q_ * 2048:(q_ + 1) * 2048], wpin_t[0:16, 0, 0:2048])
            p.load(wpin_t[0:16, 0, 0:16], fb_n, "dP")
            p.copy("V", lrfb_t[:], wpin_t[0:16, 0, 0:16])
            p.load(wpin_t[0:16, 0, 0:1024], v_n, "dP")
            p.copy("V", lrv_t[:], wpin_t[0:16, 0, 0:1024])
            for sl in range(FFI // 512):
                p.mm(PS[7][0:16, 0:512], lrfb_t[:], lru_t[0:16, sl * 512:(sl + 1) * 512])
                p.copy("A", lrfbu_t[0:16, sl * 512:(sl + 1) * 512], PS[7][0:16, 0:512])

            def make_vfbu(kc):
                for sl in range(FFI // 512):
                    p.mm(bank(6, 0, 512), lrv_t[0:16, kc * 128:(kc + 1) * 128],
                         lrfbu_t[0:16, sl * 512:(sl + 1) * 512])
                    p.copy("A", prvf_t[:, sl * 512:(sl + 1) * 512], bank(6, 0, 512))

            tern_stream(s_weffT, wfcT, 8, FFI, 1024, extra=make_vfbu)
            tern_stream(s_wpjT, wpjT, 32, 1024, FFI)

    p.barrier()
    emit_weight_prologue()
    p.barrier()

    def ttr(out_dump, in0, in1, accum, op0=OP.mult, op1=OP.add):
        p.tt("V", out_dump, in0, in1, op0)
        p.red(accum, out_dump, op1)

    OUTB = {0: (0, 1), 1: (2, 3), 2: (0, 1), 3: (2, 3)}

    def emit_ln(xsrc, xnf, xnb, lnc, dump):
        p.red(lnc[:, 0:1], xsrc, OP.add)
        ttr(dump, xsrc, xsrc, lnc[:, 1:2])
        p.ts("V", lnc[:, 2:3], lnc[:, 0:1], 1.0 / 1024, OP.mult)
        p.ts("V", lnc[:, 3:4], lnc[:, 1:2], 1.0 / 1024, OP.mult)
        p.tt("V", lnc[:, 4:5], lnc[:, 2:3], lnc[:, 2:3], OP.mult)
        p.tt("V", lnc[:, 4:5], lnc[:, 3:4], lnc[:, 4:5], OP.subtract)
        p.ts("V", lnc[:, 4:5], lnc[:, 4:5], 1e-5, OP.add)
        p.recip(lnc[:, 5:6], lnc[:, 4:5])
        p.act(lnc[:, 6:7], lnc[:, 5:6], AF.Sqrt)
        p.stt(lnc[:, 7:8], lnc[:, 2:3], -1.0, lnc[:, 6:7], OP.mult, OP.mult)
        p.ts("V", xnf, xsrc, lnc[:, 6:7], OP.mult, lnc[:, 7:8], OP.add)
        if xnb is not None:
            p.copy("A", xnb, xnf)

    def emit_trans_to_fm(src_tm, dst_fm_chunks, tc, nk, pb0=6, f32=False, kc0=0):
        for g0 in range(0, nk, 4):
            gn = min(4, nk - g0)
            pb = pb0 + (g0 // 4) % 2
            for j in range(gn):
                if f32:
                    p.trans(PS[pb][:, j * 128:(j + 1) * 128],
                            src_tm[:, (g0 + j) * 128:(g0 + j + 1) * 128], ident_f)
                else:
                    p.trans(PS[pb][:].bitcast(BF16)[:, j * 256:j * 256 + 128],
                            src_tm[:, (g0 + j) * 128:(g0 + j + 1) * 128], ident_b)
            if f32:
                p.copy("A", dst_fm_chunks[:, kc0 + g0:kc0 + g0 + gn, tc * 128:(tc + 1) * 128],
                       PS[pb][:].rearrange("p (a b) -> p a b", b=128)[:, 0:gn, :])
            else:
                p.copy("A", dst_fm_chunks[:, kc0 + g0:kc0 + g0 + gn, tc * 128:(tc + 1) * 128],
                       PS[pb][:].bitcast(BF16).rearrange("p (a b) -> p a b", b=256)[:, 0:gn, 0:128])

    def emit_tile(seq, ti, first):
        t0 = ti * TILE
        if first:
            p.memset("V", scan_carry[:], 0.0)
            p.memset("V", Sst[:].rearrange("p a b -> p (a b)"), 0.0)
            p.memset("V", qtail[:].rearrange("p a b -> p (a b)"), 0.0)
            p.memset("V", ktail[:].rearrange("p a b -> p (a b)"), 0.0)
            p.memset("V", vtail[:].rearrange("p a b -> p (a b)"), 0.0)
            p.memset("V", vconv[:, :, 0:64], 0.0)
        p.load(x_t[:], xin[seq, t0:t0 + TILE, :].rearrange("(c q) f -> q c f", q=128), "dX")
        if 0 not in layers:
            p.load(yout[seq, t0:t0 + TILE, :].rearrange("(c q) f -> q c f", q=128), x_t[:], "dY")
            return

        with ExitStack() as _ph:
            _en = _ph.enter_context
            wbc_t = _en(nc.sbuf_tensor(f"wbc_{seq}_{ti}", [128, 8, 1024], BF16))
            wcc_t = _en(nc.sbuf_tensor(f"wcc_{seq}_{ti}", [128, 8, 1024], BF16))
            wdt_t = _en(nc.sbuf_tensor(f"wdt_{seq}_{ti}", [128, 8, 1024], BF16))
            wgs_t = _en(nc.sbuf_tensor(f"wgs_{seq}_{ti}", [128, 8, 512], BF16))
            wus_t = _en(nc.sbuf_tensor(f"wus_{seq}_{ti}", [128, 8, 512], BF16))
            wds_t = _en(nc.sbuf_tensor(f"wds_{seq}_{ti}", [128, 4, 1024], BF16))
            xnf_t = _en(nc.sbuf_tensor(f"xnf0_{seq}_{ti}", [128, 1024], F32))
            xqb_t = _en(nc.sbuf_tensor(f"xqb0_{seq}_{ti}", [128, 512], BF16))
            lnd_t = _en(nc.sbuf_tensor(f"lnd0_{seq}_{ti}", [128, 1024], F32))
            lnc_t = _en(nc.sbuf_tensor(f"lnc0_{seq}_{ti}", [128, 16], F32))
            qf1_t = _en(nc.sbuf_tensor(f"qf1_{seq}_{ti}", [128, 512], F32))
            qf2_t = _en(nc.sbuf_tensor(f"qf2_{seq}_{ti}", [128, 512], F32))
            qsc3_t = _en(nc.sbuf_tensor(f"qsc3_{seq}_{ti}", [128, 3, 64], F32))
            qscb3_t = _en(nc.sbuf_tensor(f"qscb3_{seq}_{ti}", [128, 64], BF16))
            qb5_t = _en(nc.sbuf_tensor(f"qb5_{seq}_{ti}", [128, 5, 512], BF16))
            hf_t = _en(nc.sbuf_tensor(f"hf0_{seq}_{ti}", [128, 512], F32))
            hqf_t = _en(nc.sbuf_tensor(f"hqf0_{seq}_{ti}", [128, 4, 128], BF16))
            qsc3b_t = _en(nc.sbuf_tensor(f"qsc3b_{seq}_{ti}", [128, 2, 64], F32))
            qscb3b_t = _en(nc.sbuf_tensor(f"qscb3b_{seq}_{ti}", [128, 64], BF16))
            qb5b_t = _en(nc.sbuf_tensor(f"qb5b_{seq}_{ti}", [128, 5, 512], BF16))
            hfb_t = _en(nc.sbuf_tensor(f"hfb0_{seq}_{ti}", [128, 512], F32))
            ctm_t = qb5b_t.ap().rearrange("p a b -> p (a b)")[:, 0:1024]
            qtm = {"qsc": qsc3_t[:, 0, :], "qrs": qsc3_t[:, 1, :], "qscb": qscb3_t[:],
                   "qf1": qf1_t[:], "qf2": qf2_t[:],
                   "qb1": qb5_t[:, 0, :], "qb2": qb5_t[:, 1, :], "qb3": qb5_t[:, 2, :],
                   "qb4": qb5_t[:, 3, :], "qb5": qb5_t[:, 4, :]}
            qtmB = {"qsc": qsc3b_t[:, 0, :], "qrs": qsc3b_t[:, 1, :], "qscb": qscb3b_t[:],
                    "qf1": lnd_t.ap()[:, 0:512], "qf2": lnd_t.ap()[:, 512:1024],
                    "qb1": qb5b_t[:, 0, :], "qb2": qb5b_t[:, 1, :], "qb3": qb5b_t[:, 2, :],
                    "qb4": qb5b_t[:, 3, :], "qb5": qb5b_t[:, 4, :]}
            p.load(wbc_t[:], s_bcomb.rearrange("c q w -> q c w"), "dW")
            p.load(wcc_t[:], s_ccomb.rearrange("c q w -> q c w"), "dW")
            p.load(wdt_t[:], s_dT.rearrange("c q w -> q c w"), "dW")
            for tc in range(TC):
                emit_ln(x_t[:, tc, :], xnf_t[:], None, lnc_t[:], lnd_t[:])
                emit_trans_to_fm(xnf_t[:], xn_fm[:], tc, 8, f32=True)
                for hh in range(2):
                    quant_fp4(xqb_t[:], xnf_t[:, hh * 512:(hh + 1) * 512], 512, qtm)
                    emit_trans_to_fm(xqb_t[:], fm2[:], tc, 4, kc0=4 * hh)
            for s in range(NSLICES):
                w = min(N_SLICE, NI - s * N_SLICE)
                nkc = (w + 127) // 128
                p.load(wgs_t[:, :, 0:w], s_wgT.rearrange("c q w -> q c w")[:, :, s * 512:s * 512 + w], "dW")
                p.load(wus_t[:, :, 0:w], s_wuT.rearrange("c q w -> q c w")[:, :, s * 512:s * 512 + w], "dW")
                p.load(wds_t[:, 0:nkc, :], s_wdT.rearrange("c q w -> q c w")[:, 4 * s:4 * s + nkc, :], "dW")
                def f0_stage1(tc):
                    tsl = slice(tc * 128, (tc + 1) * 128)
                    gb, ub = (4, 5) if tc % 2 == 0 else (6, 7)
                    for kc in range(8):
                        p.mm(bank(gb, 0, w), fm2[:, kc, tsl], wgs_t[:, kc, 0:w],
                             start=(kc == 0), stop=(kc == 7))
                    for kc in range(8):
                        p.mm(bank(ub, 0, w), fm2[:, kc, tsl], wus_t[:, kc, 0:w],
                             start=(kc == 0), stop=(kc == 7))

                def f0_stage2(tc):
                    par = tc % 2
                    gb, ub = (4, 5) if par == 0 else (6, 7)
                    qt = (qtm, qtmB)[par]
                    hfx = (hf_t, hfb_t)[par]
                    qbx = (qb5_t, qb5b_t)[par]
                    hsg = qbx[:, 4, 0:w]
                    p.act(hsg, bank(gb, 0, w), AF.Silu)
                    p.tt("V", hfx[:, 0:w], hsg, bank(ub, 0, w), OP.mult)
                    quant_fp4(qbx[:, 2, 0:w], hfx[:, 0:w], w, qt)

                def f0_stage3(tc):
                    par = tc % 2
                    tb = 2 if par == 0 else 0
                    qbx = (qb5_t, qb5b_t)[par]
                    ob = OUTB[tc]
                    for j in range(nkc):
                        p.trans(PS[tb][:].bitcast(BF16)[:, 0:128], qbx[:, 2, j * 128:(j + 1) * 128], ident_b)
                        p.copy("A", hqf_t[:, j, :], PS[tb][:].bitcast(BF16)[:, 0:128])
                    for j in range(nkc):
                        p.mm(bank(ob[0], 0, 512), hqf_t[:, j, :], wds_t[:, j, 0:512],
                             start=(j == 0), stop=(j == nkc - 1))
                        p.mm(bank(ob[1], 0, 512), hqf_t[:, j, :], wds_t[:, j, 512:1024],
                             start=(j == 0), stop=(j == nkc - 1))
                    p.tt("V", x_t[:, tc, 0:512], x_t[:, tc, 0:512], bank(ob[0], 0, 512), OP.add)
                    p.tt("V", x_t[:, tc, 512:1024], x_t[:, tc, 512:1024], bank(ob[1], 0, 512), OP.add)

                for tc in range(TC):
                    f0_stage1(tc)
                    if tc >= 1:
                        f0_stage2(tc - 1)
                    if tc >= 2:
                        f0_stage3(tc - 2)
                f0_stage2(3)
                f0_stage3(2)
                f0_stage3(3)
            if True:
                for tc in range(TC):
                    tsl = slice(tc * 128, (tc + 1) * 128)
                    ob = OUTB[tc]
                    for kc in range(8):
                        p.mm(bank(6, 0, 512), xn_fm[:, kc, tsl], wbc_t[:, kc, 0:512],
                             start=(kc == 0), stop=(kc == 7))
                        p.mm(bank(7, 0, 512), xn_fm[:, kc, tsl], wbc_t[:, kc, 512:1024],
                             start=(kc == 0), stop=(kc == 7))
                    p.tt("V", qb5_t[:, 0, 0:512], lm_re[:], bank(6, 0, 512), OP.mult)
                    p.tt("V", qb5_t[:, 1, 0:512], lm_im[:], bank(7, 0, 512), OP.mult)
                    p.tt("V", ctm_t[:, 0:512], qb5_t[:, 0, 0:512], qb5_t[:, 1, 0:512], OP.subtract)
                    p.tt("V", qb5_t[:, 0, 0:512], lm_re[:], bank(7, 0, 512), OP.mult)
                    p.tt("V", qb5_t[:, 1, 0:512], lm_im[:], bank(6, 0, 512), OP.mult)
                    p.tt("V", ctm_t[:, 512:1024], qb5_t[:, 0, 0:512], qb5_t[:, 1, 0:512], OP.add)
                    p.tt("V", lnc_t[:, 8:12], lam_col[:, 0:4], scan_carry[:, 0:4], OP.mult)
                    p.tt("V", lnc_t[:, 12:16], lam_col[:, 4:8], scan_carry[:, 4:8], OP.mult)
                    p.tt("V", lamS[:, 0:4], lnc_t[:, 8:12], lnc_t[:, 12:16], OP.subtract)
                    p.tt("V", lnc_t[:, 8:12], lam_col[:, 0:4], scan_carry[:, 4:8], OP.mult)
                    p.tt("V", lnc_t[:, 12:16], lam_col[:, 4:8], scan_carry[:, 0:4], OP.mult)
                    p.tt("V", lamS[:, 4:8], lnc_t[:, 8:12], lnc_t[:, 12:16], OP.add)
                    for mc in range(8):
                        pb = 6 if mc < 4 else 7
                        p.mm(bank(pb, (mc % 4) * 128, 128), ctm_t[:, mc * 128:(mc + 1) * 128], tri_sb[:])
                    for mc in range(8):
                        pb = 6 if mc < 4 else 7
                        p.act(ctm_t[:, mc * 128:(mc + 1) * 128], bank(pb, (mc % 4) * 128, 128),
                              AF.Identity, bias=lamS[:, mc:mc + 1])
                    stb_t = qb5_t[:].rearrange("p a b -> p (a b)")
                    for cc in range(4):
                        zre = ctm_t[:, cc * 128:(cc + 1) * 128]
                        zim = ctm_t[:, (4 + cc) * 128:(5 + cc) * 128]
                        lre = lp_re[:, cc * 128:(cc + 1) * 128]
                        lim = lp_im[:, cc * 128:(cc + 1) * 128]
                        p.tt("V", qb5_t[:, 3, 0:128], lre, zre, OP.mult)
                        p.tt("V", qb5_t[:, 4, 0:128], lim, zim, OP.mult)
                        p.tt("V", stb_t[:, cc * 128:(cc + 1) * 128], qb5_t[:, 3, 0:128],
                             qb5_t[:, 4, 0:128], OP.subtract)
                        p.tt("V", qb5_t[:, 3, 0:128], lre, zim, OP.mult)
                        p.tt("V", qb5_t[:, 4, 0:128], lim, zre, OP.mult)
                        p.tt("V", stb_t[:, (4 + cc) * 128:(5 + cc) * 128], qb5_t[:, 3, 0:128],
                             qb5_t[:, 4, 0:128], OP.add)
                    p.copy("V", scan_carry[:, 0:8],
                           stb_t[:, 0:1024].rearrange("p (c u) -> p c u", u=128)[:, :, 127])
                    ob = OUTB[tc]
                    for nc8 in range(8):
                        p.mm(bank(ob[0], 0, 512), stb_t[:, nc8 * 128:(nc8 + 1) * 128],
                             wcc_t[:, nc8, 0:512], start=(nc8 == 0), stop=False)
                        p.mm(bank(ob[1], 0, 512), stb_t[:, nc8 * 128:(nc8 + 1) * 128],
                             wcc_t[:, nc8, 512:1024], start=(nc8 == 0), stop=False)
                    for kc in range(8):
                        p.mm(bank(ob[0], 0, 512), xn_fm[:, kc, tsl], wdt_t[:, kc, 0:512],
                             start=False, stop=(kc == 7))
                        p.mm(bank(ob[1], 0, 512), xn_fm[:, kc, tsl], wdt_t[:, kc, 512:1024],
                             start=False, stop=(kc == 7))
                    p.tt("V", x_t[:, tc, 0:512], x_t[:, tc, 0:512], bank(ob[0], 0, 512), OP.add)
                    p.tt("V", x_t[:, tc, 512:1024], x_t[:, tc, 512:1024], bank(ob[1], 0, 512), OP.add)
        p.barrier()

        if 1 not in layers:
            p.load(yout[seq, t0:t0 + TILE, :].rearrange("(c q) f -> q c f", q=128), x_t[:], "dY")
            return

        with ExitStack() as _ph:
            _en = _ph.enter_context
            wqv_t = _en(nc.sbuf_tensor(f"wqv_{seq}_{ti}", [128, 3, 8, 1024], BF16))
            wbb_t = _en(nc.sbuf_tensor(f"wbb_{seq}_{ti}", [128, 8, 4], BF16))
            xnf1_t = _en(nc.sbuf_tensor(f"xnf1_{seq}_{ti}", [128, 1024], F32))
            xnb1_t = _en(nc.sbuf_tensor(f"xnb1_{seq}_{ti}", [128, 1024], BF16))
            lnc1_t = _en(nc.sbuf_tensor(f"lnc1_{seq}_{ti}", [128, 16], F32))
            lndmp_t = _en(nc.sbuf_tensor(f"lndmp_{seq}_{ti}", [128, 1024], F32))
            cvin_t = _en(nc.sbuf_tensor(f"cvin_{seq}_{ti}", [128, 2, 8, 516], BF16))
            cacc_t = _en(nc.sbuf_tensor(f"cacc_{seq}_{ti}", [128, 3, 512], BF16))
            p.load(wqv_t[:, 0], s_wq.rearrange("c q w -> q c w"), "dW")
            p.load(wqv_t[:, 1], s_wk.rearrange("c q w -> q c w"), "dW")
            p.load(wqv_t[:, 2], s_wv.rearrange("c q w -> q c w"), "dW")
            p.load(wbb_t[:], s_wb.rearrange("c q w -> q c w"), "dW")
            for tc in range(TC):
                emit_ln(x_t[:, tc, :], xnf1_t[:], xnb1_t[:], lnc1_t[:], lndmp_t[:])
                emit_trans_to_fm(xnb1_t[:], xn_fm[:], tc, 8)
            tails = (qtail, ktail, vtail)
            for pj in range(3):
                cv = cvin_t[:, 0] if pj == 0 else (cvin_t[:, 1] if pj == 1 else cvin_t[:, 0])
                for oc in range(8):
                    pb = 4 + (oc % 2)
                    for kc in range(8):
                        p.mm(bank(pb, 0, 512), wqv_t[:, pj, kc, oc * 128:(oc + 1) * 128],
                             xn_fm[:, kc, :], start=(kc == 0), stop=(kc == 7))
                    p.copy("V", cv[:, oc, 1:4], tails[pj][:, oc, :])
                    p.copy("A", cv[:, oc, 4:516], bank(pb, 0, 512))
                    p.copy("V", tails[pj][:, oc, :], cv[:, oc, 513:516])
                    a0 = cacc_t[:, pj, :]; a1 = cacc_t[:, (pj + 1) % 3, :]
                    p.ts("V", a0, cv[:, oc, 1:513], convw_sb[:, oc, pj * 4 + 3:pj * 4 + 4], OP.mult)
                    for d in (2, 1):
                        p.ts("V", a1, cv[:, oc, 4 - d:516 - d], convw_sb[:, oc, pj * 4 + d:pj * 4 + d + 1], OP.mult)
                        p.tt("V", a0, a0, a1, OP.add)
                    p.ts("V", a1, cv[:, oc, 4:516], convw_sb[:, oc, pj * 4:pj * 4 + 1], OP.mult)
                    p.tt("V", a0, a0, a1, OP.add)
                    dst = (q_fm[:, oc, :] if pj == 0 else
                           (k_fm[:, oc, :] if pj == 1 else vconv[:, oc, 64:576]))
                    p.act(dst, a0, AF.Silu)
            for kc in range(8):
                p.mm(PS[7][0:4, 0:512], wbb_t[:, kc, :], xn_fm[:, kc, :],
                     start=(kc == 0), stop=(kc == 7))
            p.act(beta_f[:], PS[7][0:4, 0:512], AF.Sigmoid)
            for tc in range(TC):
                p.trans(PS[6][:].bitcast(BF16)[:, 0:4], beta_f[0:4, tc * 128:(tc + 1) * 128], ident_b[0:4, 0:4])
                p.copy("A", beta_tm[:, tc, :], PS[6][:].bitcast(BF16)[:, 0:4])
        p.barrier()

        with ExitStack() as _ph:
            _en = _ph.enter_context
            sqb_t = _en(nc.sbuf_tensor(f"sqb_{seq}_{ti}", [128, 512], BF16))
            rsqb_t = _en(nc.sbuf_tensor(f"rsqb_{seq}_{ti}", [4, 512], BF16))
            rskb_t = _en(nc.sbuf_tensor(f"rskb_{seq}_{ti}", [4, 512], BF16))
            kn_t = _en(nc.sbuf_tensor(f"kn_f_{seq}_{ti}", [128, 8, 512], BF16))
            rsq_f = _en(nc.sbuf_tensor(f"rsq_f_{seq}_{ti}", [4, TILE], F32))
            rsk_f = _en(nc.sbuf_tensor(f"rsk_f_{seq}_{ti}", [4, TILE], F32))
            for (src, rsb, rsf) in ((q_fm, rsqb_t, rsq_f[:]), (k_fm, rskb_t, rsk_f[:])):
                for kc in range(8):
                    p.tt("V", sqb_t[:], src[:, kc, :], src[:, kc, :], OP.mult)
                    p.mm(PS[7][0:4, 0:512], ind_sb[:, kc, :], sqb_t[:],
                         start=(kc == 0), stop=(kc == 7))
                p.ts("V", rsf, PS[7][0:4, 0:512], 1e-12, OP.add)
                p.recip(rsf, rsf)
                p.act(rsb[:], rsf, AF.Sqrt)
            for tc in range(TC):
                p.trans(PS[6][:].bitcast(BF16)[:, 0:4], rskb_t[0:4, tc * 128:(tc + 1) * 128], ident_b[0:4, 0:4])
                p.copy("A", rsk_tm[:, tc, :], PS[6][:].bitcast(BF16)[:, 0:4])
            for kc in range(8):
                h = kc // 2
                p.mm(bank(6, 0, 512), sel4[0:4, h * 128:(h + 1) * 128], rsqb_t[:])
                p.tt("V", fm2[:, kc, :], q_fm[:, kc, :], bank(6, 0, 512), OP.mult)
                p.mm(bank(5, 0, 512), sel4[0:4, h * 128:(h + 1) * 128], rskb_t[:])
                p.tt("V", kn_t[:, kc, :], k_fm[:, kc, :], bank(5, 0, 512), OP.mult)
                p.mm(bank(4, 0, 512), sel4[0:4, h * 128:(h + 1) * 128], beta_f[:])
                p.tt("V", kb_fm[:, kc, :], kn_t[:, kc, :], bank(4, 0, 512), OP.mult)
            for kc in range(8):
                p.copy("A", k_fm[:, kc, :], kn_t[:, kc, :])
        p.barrier()

        with ExitStack() as _ph:
            _en = _ph.enter_context
            w1as_t = _en(nc.sbuf_tensor(f"w1as_{seq}_{ti}", [128, 8, 1024], BF16))
            wos_t = _en(nc.sbuf_tensor(f"wos_{seq}_{ti}", [128, 8, 1024], BF16))
            wef_t = _en(nc.sbuf_tensor(f"wef_{seq}_{ti}", [128, 8, 1024], BF16))
            wpjs_t = _en(nc.sbuf_tensor(f"wpjs_{seq}_{ti}", [128, 8, 1024], BF16))
            vtm_t = _en(nc.sbuf_tensor(f"vtm_{seq}_{ti}", [128, 1024], BF16))
            kntm_t = _en(nc.sbuf_tensor(f"kntm_{seq}_{ti}", [128, 1024], BF16))
            vbtm_t = _en(nc.sbuf_tensor(f"vbtm_{seq}_{ti}", [128, 1024], BF16))
            kbtm_t = _en(nc.sbuf_tensor(f"kbtm_{seq}_{ti}", [128, 1024], BF16))
            dB_t = _en(nc.sbuf_tensor(f"dB_{seq}_{ti}", [128, 512], BF16))
            dBt_t = _en(nc.sbuf_tensor(f"dBt_{seq}_{ti}", [128, 512], BF16))
            dBq_t = _en(nc.sbuf_tensor(f"dBq_{seq}_{ti}", [128, 1, 512], BF16))
            dBqt_t = _en(nc.sbuf_tensor(f"dBqt_{seq}_{ti}", [128, 1, 512], BF16))
            dSt_t = _en(nc.sbuf_tensor(f"dSt_{seq}_{ti}", [128, 512], BF16))
            dAt_t = _en(nc.sbuf_tensor(f"dAt_{seq}_{ti}", [128, 512], BF16))
            dup_t = _en(nc.sbuf_tensor(f"dup_{seq}_{ti}", [128, 1024], BF16))
            dwn_t = _en(nc.sbuf_tensor(f"dwn_{seq}_{ti}", [128, 1024], BF16))
            sbf_t = _en(nc.sbuf_tensor(f"sbf_{seq}_{ti}", [128, 8, 256], BF16))
            stt_t = _en(nc.sbuf_tensor(f"stt__{seq}_{ti}", [128, 4, 32], BF16))
            stf4_t = _en(nc.sbuf_tensor(f"stf4_{seq}_{ti}", [32, 4, 128], BF16))
            stcol_t = _en(nc.sbuf_tensor(f"stcol_{seq}_{ti}", [128, 16], F32))
            stdmp_t = _en(nc.sbuf_tensor(f"stdmp_{seq}_{ti}", [128, 256], F32))
            prb_t = _en(nc.sbuf_tensor(f"prb_{seq}_{ti}", [128, 8], F32))
            hsa_t = _en(nc.sbuf_tensor(f"hsa_{seq}_{ti}", [128, 8, 128], BF16))
            hgb_t = _en(nc.sbuf_tensor(f"hgb_{seq}_{ti}", [128, 128], BF16))
            lgf_t = _en(nc.sbuf_tensor(f"lgf_{seq}_{ti}", [4, 512], BF16))
            smx_t = _en(nc.sbuf_tensor(f"smx_{seq}_{ti}", [128, 6], F32))
            oc2_t = _en(nc.sbuf_tensor(f"oc2_{seq}_{ti}", [128, 2, 256], BF16))
            ofm_t = _en(nc.sbuf_tensor(f"ofm_{seq}_{ti}", [128, 8, 128], BF16))
            dor2_t = _en(nc.sbuf_tensor(f"dor2_{seq}_{ti}", [128, 2048], BF16))
            fac_r = _en(nc.sbuf_tensor(f"facr_{seq}_{ti}", [128, 2, 512], BF16))
            r2g_t = dor2_t.ap().rearrange("p (a b) -> p a b", b=512)
            do_tm_t = dor2_t.ap()[:, 0:1024]
            p.load(w1as_t[:], s_w1a.rearrange("c q w -> q c w"), "dW")
            p.load(wos_t[:], s_wo.rearrange("c q w -> q c w"), "dW")
            p.memset("V", stt_t[:].rearrange("p a b -> p (a b)"), 0.0)
            accs = (dup_t.ap()[:, 0:512], dup_t.ap()[:, 512:1024])
            rsc = dwn_t.ap()[:, 0:512]

            def fir_taps(oc, ndm1, w0):
                for i, d in enumerate(range(ndm1, -1, -1)):
                    a = accs[i % 2]
                    src = vconv[:, oc, 64 - d:576 - d]
                    wcol = firw_sb[:, oc, w0 + d:w0 + d + 1]
                    if i < 2:
                        p.ts("V", a, src, wcol, OP.mult)
                    else:
                        p.stt(a, src, wcol, a, OP.mult, OP.add)

            def r2blk(b):
                for j4 in range(4):
                    ocf = (b % 2) * 4 + j4
                    for kc in range(8):
                        p.mm(PS[4 + j4][:, 0:512], wef_t[:, kc, ocf * 128:(ocf + 1) * 128],
                             xn_fm[:, kc, 0:512], start=(kc == 0), stop=(kc == 7))

            def evacblk(b):
                for j4 in range(4):
                    p.ts("V", rsc, PS[4 + j4][:], 0.0, OP.max)
                    p.tt("V", r2g_t[:, j4, :], rsc, PS[4 + j4][:], OP.mult)

            def downblk(b):
                for tc_ in range(TC):
                    fb_ = OUTB[tc_]
                    for j4 in range(4):
                        jj = (b % 2) * 4 + j4
                        p.mm(PS[fb_[0]][:, 0:512], r2g_t[:, j4, tc_ * 128:(tc_ + 1) * 128],
                             wpjs_t[:, jj, 0:512], start=(j4 == 0), stop=(j4 == 3))
                        p.mm(PS[fb_[1]][:, 0:512], r2g_t[:, j4, tc_ * 128:(tc_ + 1) * 128],
                             wpjs_t[:, jj, 512:1024], start=(j4 == 0), stop=(j4 == 3))
                    p.tt("V", x_t[:, tc_, 0:512], x_t[:, tc_, 0:512], PS[fb_[0]][:, 0:512], OP.add)
                    p.tt("V", x_t[:, tc_, 512:1024], x_t[:, tc_, 512:1024], PS[fb_[1]][:, 0:512], OP.add)

            def transblk(oc):
                for sl, dst_tm in ((0, ll_tm), (1, ls_tm)):
                    for tc_ in range(TC):
                        p.trans(PS[2][:].bitcast(BF16)[:, 0:128],
                                fac_r[:, sl, tc_ * 128:(tc_ + 1) * 128], ident_b)
                        p.copy("A", dst_tm[:, tc_, oc * 128:(oc + 1) * 128],
                               PS[2][:].bitcast(BF16)[:, 0:128])

            for k in range(9):
                pqb = k - 1
                if pqb >= 0:
                    evacblk(pqb)
                if k < 8:
                    if k % 2 == 0:
                        mg = k // 2
                        p.load(wef_t[:], s_weffT.rearrange("c q w -> q c w")[:, :, mg * 1024:(mg + 1) * 1024], "dW")
                    r2blk(k)
                    fir_taps(k, 63, 5)
                if pqb >= 0:
                    downblk(pqb)
                    transblk(pqb)
                if k < 8:
                    if k % 2 == 0:
                        p.load(wpjs_t[:], s_wpjT[(k // 2) * 8:(k // 2 + 1) * 8].rearrange("c q w -> q c w"), "dW")
                    p.tt("V", fac_r[:, 0, :], accs[0], accs[1], OP.add)
                    fir_taps(k, 4, 0)
                    p.tt("V", fac_r[:, 1, :], accs[0], accs[1], OP.add)
            for oc in range(8):
                p.copy("V", vconv[:, oc, 0:64], vconv[:, oc, 512:576])
            bfv = lambda pb: PS[pb][:].bitcast(BF16)
            for tc in range(TC):
                tsl = slice(tc * 128, (tc + 1) * 128)
                fb = OUTB[tc]
                for kc in range(8):
                    p.trans(bfv(6)[:, (kc % 4) * 256:(kc % 4) * 256 + 128],
                            vconv[:, kc, 64 + tc * 128:64 + (tc + 1) * 128], ident_b)
                    if kc % 4 == 3:
                        p.copy("A", vtm_t[:].rearrange("p (c u) -> p c u", u=128)[:, kc - 3:kc + 1, :],
                               bfv(6).rearrange("p (a b) -> p a b", b=256)[:, 0:4, 0:128])
                for kc in range(8):
                    p.trans(bfv(7)[:, (kc % 4) * 256:(kc % 4) * 256 + 128],
                            k_fm[:, kc, tsl], ident_b)
                    if kc % 4 == 3:
                        p.copy("A", kntm_t[:].rearrange("p (c u) -> p c u", u=128)[:, kc - 3:kc + 1, :],
                               bfv(7).rearrange("p (a b) -> p a b", b=256)[:, 0:4, 0:128])
                for h in range(NH):
                    p.ts("V", vbtm_t[:, h * 256:(h + 1) * 256], vtm_t[:, h * 256:(h + 1) * 256],
                         beta_tm[:, tc, h:h + 1], OP.mult)
                    p.ts("V", kbtm_t[:, h * 256:(h + 1) * 256], kntm_t[:, h * 256:(h + 1) * 256],
                         beta_tm[:, tc, h:h + 1], OP.mult)
                for i8 in range(8):
                    p.copy("A", sbf_t[:, i8, :], Sst[:, i8, :])
                for h in range(NH):
                    p.mm(PS[4][:, h * 128:(h + 1) * 128], kb_fm[:, 2 * h, tsl], k_fm[:, 2 * h, tsl],
                         start=True, stop=False)
                    p.mm(PS[4][:, h * 128:(h + 1) * 128], kb_fm[:, 2 * h + 1, tsl], k_fm[:, 2 * h + 1, tsl],
                         start=False, stop=True)
                p.tt("V", dB_t[:].rearrange("p (a b) -> p a b", b=128),
                     PS[4][:].rearrange("p (a b) -> p a b", b=128),
                     masksl[:].unsqueeze(1).broadcast_to((128, 4, 128)), OP.mult)
                for h in range(NH):
                    p.trans(bfv(5)[:, h * 256:h * 256 + 128], dB_t[:, h * 128:(h + 1) * 128], ident_b)
                p.copy("A", dBt_t[:].rearrange("p (a b) -> p a b", b=128),
                       bfv(5).rearrange("p (a b) -> p a b", b=256)[:, 0:4, 0:128])
                for h in range(NH):
                    p.mm(PS[4][:, h * 128:(h + 1) * 128], k_fm[:, 2 * h, tsl], fm2[:, 2 * h, tsl],
                         start=True, stop=False)
                    p.mm(PS[4][:, h * 128:(h + 1) * 128], k_fm[:, 2 * h + 1, tsl], fm2[:, 2 * h + 1, tsl],
                         start=False, stop=True)
                p.tt("V", dAt_t[:].rearrange("p (a b) -> p a b", b=128),
                     PS[4][:].rearrange("p (a b) -> p a b", b=128),
                     maskui[:].unsqueeze(1).broadcast_to((128, 4, 128)), OP.mult)
                p.tt("V", dSt_t[:], i4_sb[:], dBt_t[:], OP.add)
                curp, curt = dB_t[:], dBt_t[:]
                for lvl in range(6):
                    if lvl % 2 == 0:
                        nxtp = dBq_t[:, 0, :]; nxtt = dBqt_t[:, 0, :]
                    else:
                        nxtp = dB_t[:]; nxtt = dBt_t[:]
                    for h in range(NH):
                        p.mm(PS[4][:, h * 128:(h + 1) * 128], curt[:, h * 128:(h + 1) * 128],
                             curp[:, h * 128:(h + 1) * 128], start=True, stop=True)
                    p.copy("A", nxtp, PS[4][:])
                    if lvl < 5:
                        for h in range(NH):
                            p.mm(PS[5][:, h * 128:(h + 1) * 128], curp[:, h * 128:(h + 1) * 128],
                                 curt[:, h * 128:(h + 1) * 128], start=True, stop=True)
                        p.copy("A", nxtt, PS[5][:])
                    for h in range(NH):
                        p.mm(PS[6][:, h * 128:(h + 1) * 128], nxtp[:, h * 128:(h + 1) * 128],
                             dSt_t[:, h * 128:(h + 1) * 128], start=True, stop=False)
                        p.mm(PS[6][:, h * 128:(h + 1) * 128], ident_b[:],
                             dSt_t[:, h * 128:(h + 1) * 128], start=False, stop=True)
                    p.copy("A", dSt_t[:], PS[6][:])
                    curp, curt = nxtp, nxtt
                for h in range(NH):
                    pb = 6 + h // 2
                    for mc in range(2):
                        p.mm(PS[pb][:, ((h % 2) * 2 + mc) * 128:((h % 2) * 2 + mc + 1) * 128],
                             kbtm_t[:, h * 256 + mc * 128:h * 256 + (mc + 1) * 128],
                             dSt_t[:, h * 128:(h + 1) * 128], start=True, stop=True)
                for pb in (6, 7):
                    p.act(dwn_t[:, (pb - 6) * 512:(pb - 5) * 512], PS[pb][:], AF.Identity, scale=-1.0)
                for h in range(NH):
                    pb = 4 + h // 2
                    off = (h % 2) * 256
                    p.mm(PS[pb][:, off:off + 256], dSt_t[:, h * 128:(h + 1) * 128],
                         vbtm_t[:, h * 256:(h + 1) * 256], start=True, stop=False)
                    for mc in range(2):
                        p.mm(PS[pb][:, off:off + 256], dwn_t[:, (h * 2 + mc) * 128:(h * 2 + mc + 1) * 128],
                             sbf_t[:, h * 2 + mc, :], start=False, stop=(mc == 1))
                p.copy("A", dup_t[:, 0:512], PS[4][:])
                p.copy("A", dup_t[:, 512:1024], PS[5][:])
                for h in range(NH):
                    pb = 4 + h // 2
                    off = (h % 2) * 256
                    p.mm(PS[pb][:, off:off + 256], fm2[:, 2 * h, tsl], sbf_t[:, h * 2, :],
                         start=True, stop=False)
                    p.mm(PS[pb][:, off:off + 256], fm2[:, 2 * h + 1, tsl], sbf_t[:, h * 2 + 1, :],
                         start=False, stop=False)
                    p.mm(PS[pb][:, off:off + 256], dAt_t[:, h * 128:(h + 1) * 128],
                         dup_t[:, h * 256:(h + 1) * 256], start=False, stop=True)
                p.copy("A", do_tm_t[:, 0:512], PS[4][:])
                p.copy("A", do_tm_t[:, 512:1024], PS[5][:])
                for hh in range(2):
                    for h in (2 * hh, 2 * hh + 1):
                        pb = 6 + (h % 2)
                        for mc in range(2):
                            p.mm(PS[pb][:, mc * 256:(mc + 1) * 256],
                                 kntm_t[:, h * 256 + mc * 128:h * 256 + (mc + 1) * 128],
                                 dup_t[:, h * 256:(h + 1) * 256], start=True, stop=True)
                    for h in (2 * hh, 2 * hh + 1):
                        pb = 6 + (h % 2)
                        for mc in range(2):
                            p.tt("V", Sst[:, h * 2 + mc, :], Sst[:, h * 2 + mc, :],
                                 PS[pb][:, mc * 256:(mc + 1) * 256], OP.add)
                srcs = (ls_tm[:, tc, :], ll_tm[:, tc, :], do_tm_t[:], vtm_t[:])
                for ti_ in range(4):
                    src = srcs[ti_]
                    for h in range(NH):
                        sl = src.rearrange("p (h d) -> p h d", d=256)[:, h, :]
                        c0 = stcol_t[:, 0:1]; c1 = stcol_t[:, 1:2]; c2 = stcol_t[:, 2:3]
                        p.red(c0, sl, OP.add)
                        ttr(stdmp_t[:], sl, sl, c1)
                        p.red(c2, sl, OP.add, absval=True)
                        p.ts("V", smx_t[:, 0:1], c0, 1.0 / 256, OP.mult)
                        p.ts("V", smx_t[:, 1:2], c1, 1.0 / 256, OP.mult)
                        p.tt("V", smx_t[:, 2:3], smx_t[:, 0:1], smx_t[:, 0:1], OP.mult)
                        p.tt("V", smx_t[:, 2:3], smx_t[:, 1:2], smx_t[:, 2:3], OP.subtract)
                        p.ts("V", smx_t[:, 3:4], c2, 1.0 / 256, OP.mult)
                        p.act(smx_t[:, 4:5], c1, AF.Sqrt)
                        p.copy("V", stt_t[:, h, ti_ * 4:ti_ * 4 + 1], smx_t[:, 0:1])
                        p.copy("V", stt_t[:, h, ti_ * 4 + 1:ti_ * 4 + 2], smx_t[:, 2:3])
                        p.copy("V", stt_t[:, h, ti_ * 4 + 2:ti_ * 4 + 3], smx_t[:, 3:4])
                        p.copy("V", stt_t[:, h, ti_ * 4 + 3:ti_ * 4 + 4], smx_t[:, 4:5])
                for h in range(NH):
                    p.trans(bfv(6)[0:32, h * 256:h * 256 + 128], stt_t[:, h, :], ident_b)
                p.copy("A", stf4_t[:],
                       bfv(6).rearrange("p (h b) -> p h b", b=256)[0:32, :, 0:128])
                for g4 in range(2):
                    pb = 4 + g4
                    for j in range(4):
                        oc = g4 * 4 + j
                        for kc in range(8):
                            p.mm(PS[pb][:, j * 128:(j + 1) * 128], w1as_t[:, kc, oc * 128:(oc + 1) * 128],
                                 xn_fm[:, kc, tsl], start=(kc == 0), stop=(kc == 7))
                    p.copy("A", hsa_t[:, g4 * 4:g4 * 4 + 4, :],
                           PS[pb][:].rearrange("p (a b) -> p a b", b=128))
                for h in range(NH):
                    for g4 in range(2):
                        pb = 4 + g4
                        for j in range(4):
                            oc = g4 * 4 + j
                            p.mm(PS[pb][:, j * 128:(j + 1) * 128], stf4_t[0:16, h, :],
                                 w1b_sb[:, oc * 128:(oc + 1) * 128], start=True, stop=True)
                    for oc in range(8):
                        pb = 4 + oc // 4
                        j = oc % 4
                        p.tt("V", hgb_t[:], PS[pb][:, j * 128:(j + 1) * 128], hsa_t[:, oc, :], OP.add)
                        p.act(hgb_t[:], hgb_t[:], AF.Gelu, bias=b1_sb[:, oc:oc + 1])
                        p.mm(PS[7][0:4, tc * 0 + h * 128:h * 128 + 128], w2_sb[:, oc, :], hgb_t[:],
                             start=(oc == 0), stop=(oc == 7))
                p.act(lgf_t[:], PS[7][0:4, 0:512], AF.Identity, scale=rtemp_sb[:], bias=b2rt_sb[:])
                for h in range(NH):
                    p.trans(PS[6][:].bitcast(BF16)[:, 0:4], lgf_t[0:4, h * 128:(h + 1) * 128], ident_b[0:4, 0:4])
                    p.copy("V", smx_t[:, 0:4], PS[6][:].bitcast(BF16)[:, 0:4])
                    p.red(smx_t[:, 4:5], smx_t[:, 0:4], OP.max)
                    p.ts("V", smx_t[:, 0:4], smx_t[:, 0:4], smx_t[:, 4:5], OP.subtract)
                    p.act(smx_t[:, 0:4], smx_t[:, 0:4], AF.Exp)
                    p.red(smx_t[:, 4:5], smx_t[:, 0:4], OP.add)
                    p.recip(smx_t[:, 4:5], smx_t[:, 4:5])
                    p.ts("V", smx_t[:, 0:4], smx_t[:, 0:4], smx_t[:, 4:5], OP.mult, 1.0 - 2 * 0.08, OP.mult)
                    p.ts("V", prb_t[:, 4 * 0:1] if False else prb_t[:, 0:1],
                         smx_t[:, 0:1], 0.08, OP.add)
                    p.tt("V", prb_t[:, 0:1], prb_t[:, 0:1], crl_sb[:, h:h + 1], OP.add)
                    p.ts("V", prb_t[:, 1:2], smx_t[:, 1:2], 0.08, OP.add)
                    p.copy("V", prb_t[:, 2:3], smx_t[:, 2:3])
                    p.copy("V", prb_t[:, 3:4], smx_t[:, 3:4])
                    hs4 = slice(h * 256, (h + 1) * 256)
                    p.ts("V", oc2_t[:, 0, :], ls_tm[:, tc, hs4], prb_t[:, 0:1], OP.mult)
                    p.ts("V", oc2_t[:, 1, :], ll_tm[:, tc, hs4], prb_t[:, 1:2], OP.mult)
                    p.tt("V", oc2_t[:, 0, :], oc2_t[:, 0, :], oc2_t[:, 1, :], OP.add)
                    p.ts("V", oc2_t[:, 1, :], do_tm_t[:, hs4], prb_t[:, 2:3], OP.mult)
                    p.tt("V", oc2_t[:, 0, :], oc2_t[:, 0, :], oc2_t[:, 1, :], OP.add)
                    p.ts("V", oc2_t[:, 1, :], vtm_t[:, hs4], prb_t[:, 3:4], OP.mult)
                    p.tt("V", oc2_t[:, 0, :], oc2_t[:, 0, :], oc2_t[:, 1, :], OP.add)
                    ttr(stdmp_t[:], oc2_t[:, 0, :], oc2_t[:, 0, :], smx_t[:, 5:6])
                    p.ts("V", smx_t[:, 5:6], smx_t[:, 5:6], 1.0 / 256, OP.mult, 1e-5, OP.add)
                    p.recip(smx_t[:, 5:6], smx_t[:, 5:6])
                    p.act(smx_t[:, 5:6], smx_t[:, 5:6], AF.Sqrt)
                    p.stt(oc2_t[:, 1, :], oc2_t[:, 0, :], smx_t[:, 5:6], rmsw_sb[:, hs4],
                          OP.mult, OP.mult)
                    for mc in range(2):
                        p.trans(bfv(6)[:, mc * 256:mc * 256 + 128],
                                oc2_t[:, 1, mc * 128:(mc + 1) * 128], ident_b)
                    p.copy("A", ofm_t[:, 2 * h:2 * h + 2, :],
                           bfv(6).rearrange("p (a b) -> p a b", b=256)[:, 0:2, 0:128])
                for kc in range(8):
                    p.mm(PS[fb[0]][:, 0:512], ofm_t[:, kc, :], wos_t[:, kc, 0:512],
                         start=(kc == 0), stop=(kc == 7))
                    p.mm(PS[fb[1]][:, 0:512], ofm_t[:, kc, :], wos_t[:, kc, 512:1024],
                         start=(kc == 0), stop=(kc == 7))
                p.tt("V", x_t[:, tc, 0:512], x_t[:, tc, 0:512], PS[fb[0]][:, 0:512], OP.add)
                p.tt("V", x_t[:, tc, 512:1024], x_t[:, tc, 512:1024], PS[fb[1]][:, 0:512], OP.add)
        p.barrier()
        p.load(yout[seq, t0:t0 + TILE, :].rearrange("(c q) f -> q c f", q=128), x_t[:], "dY")

    for seq in range(nseqs):
        for ti in range(ntiles):
            emit_tile(seq, ti, ti == 0)
    p.qwait("S", "dY", p.dmacnt.get("dY", 0))
    p.finalize()

    with nc.Block() as block:
        @block.sync
        def _(eng):
            for f in p.ops["S"]:
                f(eng)

        @block.vector
        def _(eng):
            for f in p.ops["V"]:
                f(eng)

        @block.scalar
        def _(eng):
            for f in p.ops["A"]:
                f(eng)

        @block.tensor
        def _(eng):
            for f in p.ops["T"]:
                f(eng)

        @block.gpsimd
        def _(eng):
            for f in p.ops["G"]:
                f(eng)
    es.close()
    return nc


_NC_CACHE = {}


def kernel(**inputs):
    key = "full"
    if key not in _NC_CACHE:
        _NC_CACHE[key] = build_program()
    nc = _NC_CACHE[key]
    in_maps = [prep_core_inputs(inputs, c) for c in range(NCORES)]
    res = run_bass_kernel_spmd(nc, in_maps, core_ids=list(range(NCORES)))
    out = np.empty((B, L, D), np.float32)
    for c in range(NCORES):
        out[2 * c:2 * c + 2] = res.results[c]["yout"]
    return out
